# version 02
# baseline (speedup 1.0000x reference)
import math
import numpy as np
import concourse.bass as bass
import concourse.mybir as mybir
from concourse.bass_utils import run_bass_kernel_spmd

F32 = mybir.dt.float32
BF16 = mybir.dt.bfloat16
I32 = mybir.dt.int32
AF = mybir.ActivationFunctionType
ALU = mybir.AluOpType
AX = mybir.AxisListType

N_CORES = 8
D = 1024
B_SZ = 4
S_LEN = 4096
T_ALL = B_SZ * S_LEN
T_CORE = T_ALL // N_CORES


def _ap_span(ap):
    lo = ap.offset
    span = 1
    for step, count in ap.ap:
        span += abs(step) * (count - 1)
    return (lo, lo + span)


class Op:
    __slots__ = ("engine", "emit", "reads", "writes", "inc", "waits", "clock",
                 "done_val", "tag", "stream")

    def __init__(self, engine, emit, reads, writes, inc, tag=""):
        self.engine = engine
        self.emit = emit
        self.reads = reads
        self.writes = writes
        self.inc = inc
        self.waits = []
        self.clock = {}
        self.done_val = 0
        self.tag = tag


class Prog:

    ENGINES = ("tensor", "vector", "scalar", "gpsimd", "sync")

    def __init__(self, name="prog"):
        self.nc = bass.Bass("TRN2", target_bir_lowering=False, debug=False)
        self.ops = []
        self.counts = {}
        self.access = {}
        self._tensors = {}
        self._ctxs = []
        self._cm_of = {}
        self._const_cache = {}
        self._freed_hazards = set()
        self._initial_deps = {}
        self._alloc_idx = {}
        self._alloc_counter = 0
        self._psum_names = set()
        self.name = name

    def dram(self, name, shape, dtype, kind="Internal"):
        t = self.nc.dram_tensor(name, list(shape), dtype, kind=kind)
        self._tensors[name] = t
        return t

    def _enter(self, cm):
        h = cm.__enter__()
        self._ctxs.append(cm)
        return h

    def sbuf(self, name, shape, dtype):
        cm = self.nc.sbuf_tensor(name, list(shape), dtype)
        h = cm.__enter__()
        self._cm_of[name] = cm
        self._tensors[name] = h
        self._alloc_counter += 1
        self._alloc_idx[name] = self._alloc_counter
        if self._freed_hazards:
            self._initial_deps[name] = frozenset(self._freed_hazards)
        return h

    def psum(self, name, shape, dtype):
        cm = self.nc.psum_tensor(name, list(shape), dtype)
        h = cm.__enter__()
        self._cm_of[name] = cm
        self._tensors[name] = h
        self._psum_names.add(name)
        self._alloc_counter += 1
        self._alloc_idx[name] = self._alloc_counter
        if self._freed_hazards:
            self._initial_deps[name] = frozenset(self._freed_hazards)
        return h

    def free(self, *handles):
        names = []
        for h in handles:
            names.append(h.name if hasattr(h, "name") else h.tensor.name)
        names.sort(key=lambda n: -self._alloc_idx.get(n, -1))
        for name in names:
            cm = self._cm_of.pop(name, None)
            if cm is None:
                continue
            for (_w, _iv, idx) in self.access.pop(name, ()):
                self._freed_hazards.add(idx)
            self._initial_deps.pop(name, None)
            self._tensors.pop(name, None)
            self._alloc_idx.pop(name, None)
            cm.__exit__(None, None, None)

    def sem(self, name):
        return self._enter(self.nc.semaphore(name))

    def _record(self, op_idx, aps, is_write):
        for ap in aps:
            name = ap.tensor.name
            iv = _ap_span(ap)
            self.access.setdefault(name, []).append((is_write, iv, op_idx))

    def _deps_for(self, reads, writes, engine=None):
        deps = set()
        for aps, against_reads in ((reads, False), (writes, True)):
            for ap in aps:
                name = ap.tensor.name
                ini = self._initial_deps.get(name)
                if ini:
                    deps.update(ini)
                if name not in self.access:
                    continue
                lo, hi = _ap_span(ap)
                psum_rr = (not against_reads) and name in self._psum_names
                for (w, (alo, ahi), idx) in self.access[name]:
                    if (w or against_reads or
                            (psum_rr and self.ops[idx].engine != engine)) and                             alo < hi and lo < ahi:
                        deps.add(idx)
        return deps

    def op(self, engine, emit, reads=(), writes=(), inc=1, tag="", stream=None):
        if stream is None:
            stream = engine + ("_d" if inc == 16 else "_c")
        reads = [r for r in reads if r is not None]
        writes = [w for w in writes if w is not None]
        o = Op(engine, emit, reads, writes, inc, tag)
        o.stream = stream
        idx = len(self.ops)
        prev_clock = {}
        for prev in reversed(self.ops):
            if prev.engine == engine:
                prev_clock = dict(prev.clock)
                break
        o.clock = prev_clock
        deps = self._deps_for(reads, writes, engine=engine)
        for dep_idx in sorted(deps):
            dep = self.ops[dep_idx]
            if dep.engine == "tensor" and engine == "tensor":
                continue
            if o.clock.get(dep.stream, 0) >= dep.done_val:
                continue
            o.waits.append((dep.stream, dep.done_val))
            for k, v in dep.clock.items():
                if o.clock.get(k, 0) < v:
                    o.clock[k] = v
            if o.clock.get(dep.stream, 0) < dep.done_val:
                o.clock[dep.stream] = dep.done_val
        wd = {}
        for e, v in o.waits:
            wd[e] = max(wd.get(e, 0), v)
        o.waits = sorted(wd.items())
        self.counts[stream] = self.counts.get(stream, 0) + inc
        o.done_val = self.counts[stream]
        self.ops.append(o)
        self._record(idx, reads, False)
        self._record(idx, writes, True)
        return o

    serial_dma = True

    def dma(self, q, out, in_, tag=""):
        assert q in self.ENGINES
        o = self.op(q, lambda e: e.dma_start(out=out, in_=in_), reads=[in_],
                    writes=[out], inc=16, tag=tag or "dma")
        if self.serial_dma:
            prev = o.done_val - 16
            if prev > 0:
                wd = dict(o.waits)
                wd[o.stream] = max(wd.get(o.stream, 0), prev)
                o.waits = sorted(wd.items())

    def mm(self, out, lhsT, rhs, start=True, stop=True, tag=""):
        self.op("tensor",
                lambda e: e.matmul(out, lhsT, rhs, start=start, stop=stop),
                reads=[lhsT, rhs], writes=[out], tag=tag or "mm")

    def transpose(self, out, in_, tag=""):
        ident = self.const_identity(in_.dtype)
        k = in_.shape[0]
        iap = ident[0:k, 0:k]
        self.op("tensor", lambda e: e.transpose(out, in_, iap),
                reads=[in_, iap], writes=[out], tag=tag or "tr")

    def act(self, out, in_, func, bias=0.0, scale=1.0, accum_out=None, tag=""):
        reads = [in_]
        for v, skip in ((bias, 0.0), (scale, 1.0)):
            if isinstance(v, bass.AP):
                reads.append(v)
            elif float(v) != skip:
                reads.append(self.fconst(v, F32))
                reads.append(self.fconst(v, BF16))
        writes = [out] + ([accum_out] if accum_out is not None else [])

        def emit(e):
            kw = {}
            if accum_out is not None:
                kw["accum_out"] = accum_out
            return e.activation(out, in_, func, bias=bias, scale=scale, **kw)
        self.op("scalar", emit, reads=reads, writes=writes, tag=tag or "act")

    def tt(self, out, a, b, alu, engine="vector", tag=""):
        self.op(engine, lambda e: e.tensor_tensor(out, a, b, alu),
                reads=[a, b], writes=[out], tag=tag or "tt")

    def ts(self, out, in0, s1, s2=None, op0=ALU.mult, op1=None, engine="vector",
           accum_out=None, tag=""):
        reads = [in0]
        if isinstance(s1, bass.AP):
            reads.append(s1)
        if isinstance(s2, bass.AP):
            reads.append(s2)
        writes = [out] + ([accum_out] if accum_out is not None else [])

        def emit(e):
            kw = dict(op0=op0)
            if op1 is not None:
                kw["op1"] = op1
            if accum_out is not None:
                kw["accum_out"] = accum_out
            return e.tensor_scalar(out, in0, s1, s2, **kw)
        self.op(engine, emit, reads=reads, writes=writes, tag=tag or "ts")

    def stt(self, out, in0, scalar, in1, op0, op1, engine="vector", tag=""):
        reads = [in0, in1]
        if isinstance(scalar, bass.AP):
            reads.append(scalar)
        self.op(engine,
                lambda e: e.scalar_tensor_tensor(out, in0, scalar, in1, op0, op1),
                reads=reads, writes=[out], tag=tag or "stt")

    def scan(self, out, data0, data1, initial, op0, op1, engine="vector", tag=""):
        reads = [data0, data1]
        if isinstance(initial, bass.AP):
            reads.append(initial)
        self.op(engine,
                lambda e: e.tensor_tensor_scan(out, data0, data1, initial, op0, op1),
                reads=reads, writes=[out], tag=tag or "scan")

    def red(self, out, in_, alu, axis=AX.X, engine="vector", negate=None,
            absolute=None, tag=""):
        def emit(e):
            kw = {}
            if negate is not None:
                kw["negate"] = negate
            if absolute is not None:
                kw["apply_absolute_value"] = absolute
            return e.tensor_reduce(out, in_, axis, alu, **kw)
        self.op(engine, emit, reads=[in_], writes=[out], tag=tag or "red")

    def copy(self, out, in_, engine="vector", tag=""):
        if engine == "scalar":
            self.op(engine, lambda e: e.copy(out, in_), reads=[in_],
                    writes=[out], tag=tag or "copy")
        else:
            self.op(engine, lambda e: e.tensor_copy(out, in_), reads=[in_],
                    writes=[out], tag=tag or "copy")

    def memset(self, ap, val, engine="vector", tag=""):
        self.op(engine, lambda e: e.memset(ap, val), reads=[], writes=[ap],
                tag=tag or "memset")

    def iota(self, out, pattern, base=None, channel_multiplier=None, tag=""):
        self.op("gpsimd",
                lambda e: e.iota(out, pattern, base=base,
                                 channel_multiplier=channel_multiplier,
                                 allow_small_or_imprecise_dtypes=True),
                reads=[], writes=[out], tag=tag or "iota")

    def recip(self, out, in_, tag=""):
        self.op("vector", lambda e: e.reciprocal(out, in_), reads=[in_],
                writes=[out], tag=tag or "recip")

    def select(self, out, mask, on_true, on_false, tag=""):
        self.op("vector", lambda e: e.select(out, mask, on_true, on_false),
                reads=[mask, on_true, on_false], writes=[out], tag=tag or "sel")

    def max8(self, out, in_, tag=""):
        self.op("vector", lambda e: e.max(out=out, in_=in_), reads=[in_],
                writes=[out], tag=tag or "max8")

    def collective(self, kind, alu, replica_groups, ins, outs, tag=""):
        self.op("gpsimd",
                lambda e: e.collective_compute(kind, alu, replica_groups=replica_groups,
                                               ins=[a.opt() for a in ins],
                                               outs=[a.opt() for a in outs]),
                reads=list(ins), writes=list(outs), inc=16, tag=tag or kind,
                stream="gpsimd_cc")

    def prealloc_consts(self):
        self._fpool = {
            F32: [self.sbuf("c_fpool_f32", [128, 64], F32), 0, {}],
            BF16: [self.sbuf("c_fpool_bf16", [128, 64], BF16), 0, {}],
        }
        self.const_identity(BF16)
        self.const_identity(F32)
        self.const_onehot16()
        self.const_ones(BF16, 128, 1)
        self.const_ones(F32, 1, 128)
        self.const_ones(F32, 128, 1)
        self.const_ones(BF16, 1, 128)

    def fconst(self, value, dtype):
        pool = self._fpool[dtype]
        t, nxt, cache = pool
        v = float(value)
        if v in cache:
            return t[:, cache[v]:cache[v] + 1]
        col = pool[1]
        assert col < 64, "fconst pool exhausted"
        pool[1] = col + 1
        cache[v] = col
        ap = t[:, col:col + 1]
        self.memset(ap, v, tag=f"fconst_{v}")
        self.nc.const_aps.register(dtype, v, ap)
        return ap

    def const_identity(self, dtype):
        key = ("ident", dtype)
        if key not in self._const_cache:
            h = self.sbuf(f"c_ident_{dtype}", [128, 128], dtype)
            self.iota(h[:, :], [[1, 128]], base=0, channel_multiplier=0, tag="ident_i")
            iot2 = self.sbuf(f"c_ident2_{dtype}", [128, 128], F32)
            self.iota(iot2[:, :], [[0, 128]], base=0, channel_multiplier=1,
                      tag="ident_i2")
            tmp = self.sbuf(f"c_ident3_{dtype}", [128, 128], F32)
            self.iota(tmp[:, :], [[1, 128]], base=0, channel_multiplier=0,
                      tag="ident_i3")
            self.tt(h[:, :], tmp[:, :], iot2[:, :], ALU.is_equal, tag="ident_eq")
            self._const_cache[key] = h
        return self._const_cache[key]

    def const_onehot16(self):
        key = "onehot16"
        if key not in self._const_cache:
            h = self.sbuf("c_onehot16", [16, 16 * 128], F32)
            rid = self.sbuf("c_onehot16_r", [16, 16 * 128], F32)
            self.iota(h[:, :].rearrange("k (e p) -> k e p", p=128),
                      [[1, 16], [0, 128]], base=0, channel_multiplier=0)
            self.iota(rid[:, :], [[0, 16 * 128]], base=0, channel_multiplier=1)
            self.tt(h[:, :], h[:, :], rid[:, :], ALU.is_equal)
            self.free(rid)
            self._const_cache[key] = h
        return self._const_cache[key]

    def const_ones(self, dtype, rows=128, cols=1):
        key = ("ones", dtype, rows, cols)
        if key not in self._const_cache:
            h = self.sbuf(f"c_ones_{dtype}_{rows}_{cols}", [rows, cols], dtype)
            self.memset(h[:, :], 1.0, tag="ones")
            self._const_cache[key] = h
        return self._const_cache[key]

    def build(self):
        nc = self.nc
        sems = {}
        blk = self.nc.Block()
        blk.__enter__()
        self._ctxs.append(blk)
        for st in sorted(self.counts):
            sems[st] = self._enter(self.nc.semaphore(f"sem_{st}"))
        by_engine = {e: [] for e in self.ENGINES}
        for o in self.ops:
            by_engine[o.engine].append(o)

        def section(eng_name):
            eng_ops = by_engine[eng_name]

            def run(handle):
                waited = {}
                for o in eng_ops:
                    for we, wv in o.waits:
                        if waited.get(we, 0) >= wv:
                            continue
                        handle.wait_ge(sems[we], wv)
                        waited[we] = wv
                    inst = o.emit(handle)
                    if inst is not None:
                        inst.then_inc(sems[o.stream], o.inc)
                    else:
                        raise RuntimeError(f"op {o.tag} returned no instruction")
                    for we, wv in o.waits:
                        pass
            return run

        for e in self.ENGINES:
            run = section(e)
            dec = getattr(blk, e if e != "sync" else "sync")
            dec(lambda handle, _run=run: _run(handle))
        return nc

    def finalize_tail(self):
        final = dict(self.counts)

        def emit(e):
            return e.nop()
        o = Op("gpsimd", emit, [], [], 1, "tail")
        o.stream = "gpsimd_c"
        o.waits = sorted((st, v) for st, v in final.items() if st != "gpsimd_c")
        self.counts["gpsimd_c"] = self.counts.get("gpsimd_c", 0) + 1
        o.done_val = self.counts["gpsimd_c"]
        self.ops.append(o)



def ln_fm(p, pre, x_src, T, xn_bf=None, xn_f32=None, eps=1e-5, nchunk=512,
          x_keep=None, xn_dram=None):
    nchunk = min(nchunk, T)
    nc = T // nchunk
    assert nc * nchunk == T
    ones_bf = p.const_ones(BF16, 128, 1)
    ones_f1 = p.const_ones(F32, 1, 128)
    xt = [p.sbuf(f"{pre}_xt{i}", [128, nchunk], F32) for i in range(2)]
    xb = p.sbuf(f"{pre}_xb", [128, nchunk], BF16)
    xq = p.sbuf(f"{pre}_xq", [128, nchunk], BF16)
    st0 = p.psum(f"{pre}_st0", [1, nchunk], F32)
    st1 = p.psum(f"{pre}_st1", [1, nchunk], F32)
    bc0 = p.psum(f"{pre}_bc0", [128, nchunk], F32)
    bc1 = p.psum(f"{pre}_bc1", [128, nchunk], F32)
    mean = p.sbuf(f"{pre}_mean", [1, nchunk], F32)
    nmean = p.sbuf(f"{pre}_nmean", [1, nchunk], F32)
    var = p.sbuf(f"{pre}_var", [1, nchunk], F32)
    rstd = p.sbuf(f"{pre}_rstd", [1, nchunk], F32)
    tmp = p.sbuf(f"{pre}_tmp", [128, nchunk], F32)
    for c in range(nc):
        cs = slice(c * nchunk, (c + 1) * nchunk)
        for k in range(8):
            dst = x_keep[k][:, cs] if x_keep is not None else xt[k % 2][:, :]
            p.dma("sync", dst, x_src[k][:, cs])
            p.copy(xb[:, :], dst)
            p.act(xq[:, :], dst, AF.Square)
            p.mm(st0[0:1, :], ones_bf[:, :], xb[:, :], start=(k == 0), stop=(k == 7))
            p.mm(st1[0:1, :], ones_bf[:, :], xq[:, :], start=(k == 0), stop=(k == 7))
        p.ts(mean[:, :], st0[0:1, :], 1.0 / 1024.0, op0=ALU.mult)
        p.ts(var[:, :], st1[0:1, :], 1.0 / 1024.0, op0=ALU.mult)
        p.ts(nmean[:, :], mean[:, :], -1.0, op0=ALU.mult)
        p.tt(tmp[0:1, :], mean[:, :], mean[:, :], ALU.mult)
        p.tt(var[:, :], var[:, :], tmp[0:1, :], ALU.subtract)
        p.act(rstd[:, :], var[:, :], AF.Sqrt, bias=float(eps))
        p.recip(rstd[:, :], rstd[:, :])
        p.mm(bc0[:, :nchunk], ones_f1[:, :], nmean[:, :], tag="bc_m")
        p.mm(bc1[:, :nchunk], ones_f1[:, :], rstd[:, :], tag="bc_r")
        for k in range(8):
            if x_keep is not None:
                src = x_keep[k][:, cs]
            else:
                src = xt[k % 2][:, :]
                p.dma("sync", src, x_src[k][:, cs])
            p.tt(tmp[:, :], src, bc0[:, :], ALU.add)
            if xn_f32 is not None:
                p.tt(xn_f32[k][:, cs], tmp[:, :], bc1[:, :], ALU.mult)
                if xn_bf is not None:
                    p.copy(xn_bf[k][:, cs], xn_f32[k][:, cs])
                if xn_dram is not None:
                    p.dma("gpsimd", xn_dram.ap()[k * 128:(k + 1) * 128, cs],
                          xn_f32[k][:, cs])
            elif xn_bf is not None:
                dst = xn_bf[k][:, cs]
                p.tt(dst, tmp[:, :], bc1[:, :], ALU.mult)
                if xn_dram is not None:
                    p.dma("gpsimd", xn_dram.ap()[k * 128:(k + 1) * 128, cs], dst)
            else:
                p.tt(xb[:, :], tmp[:, :], bc1[:, :], ALU.mult)
                p.dma("gpsimd", xn_dram.ap()[k * 128:(k + 1) * 128, cs], xb[:, :])
    p.free(xt[0], xt[1], xb, xq, st0, st1, bc0, bc1, mean, nmean, var, rstd, tmp)


def emit_quantize_w(p, pre, w_dram_list, eps=1e-5):
    ones_f1 = p.const_ones(F32, 1, 128)
    ones_f128 = p.const_ones(F32, 128, 1)
    for wi, (w, qw, rows, cols) in enumerate(w_dram_list):
        nk = rows // 128
        wt = p.sbuf(f"{pre}_wt{wi}", [128, cols], F32)
        acc = p.sbuf(f"{pre}_acc{wi}", [128, 1], F32)
        red = p.sbuf(f"{pre}_red{wi}", [128, 1], F32)
        gps = p.psum(f"{pre}_gps{wi}", [1, 1], F32)
        rg = p.sbuf(f"{pre}_rg{wi}", [1, 1], F32)
        bcps = p.psum(f"{pre}_bcps{wi}", [128, 1], F32)
        rg128 = p.sbuf(f"{pre}_rg128_{wi}", [128, 1], F32)
        qt = p.sbuf(f"{pre}_qt{wi}", [128, cols], BF16)
        for k in range(nk):
            p.dma("sync", wt[:, :], w.ap()[k * 128:(k + 1) * 128, :])
            p.red(red[:, :], wt[:, :], ALU.add, absolute=True, tag="absred")
            if k == 0:
                p.copy(acc[:, :], red[:, :])
            else:
                p.tt(acc[:, :], acc[:, :], red[:, :], ALU.add)
        p.mm(gps[:, :], acc[:, :], ones_f128[:, :], tag="gsum")
        p.ts(rg[:, :], gps[:, :], 1.0 / float(rows * cols), float(eps),
             op0=ALU.mult, op1=ALU.add)
        p.recip(rg[:, :], rg[:, :])
        p.mm(bcps[:, :], ones_f1[:, :], rg[:, :], tag="gbc")
        p.copy(rg128[:, :], bcps[:, :])
        pos = p.sbuf(f"{pre}_pos{wi}", [128, cols], F32)
        for k in range(nk):
            p.dma("sync", wt[:, :], w.ap()[k * 128:(k + 1) * 128, :])
            p.ts(wt[:, :], wt[:, :], rg128[:, 0:1], op0=ALU.mult)
            p.ts(pos[:, :], wt[:, :], 0.5, op0=ALU.is_gt)
            p.ts(wt[:, :], wt[:, :], -0.5, op0=ALU.is_lt)
            p.tt(qt[:, :], pos[:, :], wt[:, :], ALU.subtract)
            p.dma("gpsimd", qw.ap()[k * 128:(k + 1) * 128, :], qt[:, :])
        p.free(pos)
        p.free(wt, acc, red, gps, rg, bcps, rg128, qt)


def emit_l1_ffn(p, pre, x_rows, out_rows, qw1, qw2, b1s_dram, b2s_dram, T):
    NC = 512
    w1s = p.sbuf(f"{pre}_w1s", [128, 8 * 4096], BF16)
    w2s = p.sbuf(f"{pre}_w2s", [128, 32 * 1024], BF16)
    b1s = p.sbuf(f"{pre}_b1s", [128, 32], F32)
    b2s = p.sbuf(f"{pre}_b2s", [128, 8], F32)
    for k in range(8):
        p.dma("sync", w1s[:, k * 4096:(k + 1) * 4096], qw1.ap()[k * 128:(k + 1) * 128, :])
    for f in range(32):
        p.dma("sync", w2s[:, f * 1024:(f + 1) * 1024], qw2.ap()[f * 128:(f + 1) * 128, :])
    p.dma("sync", b1s[:, :], b1s_dram.ap())
    p.dma("sync", b2s[:, :], b2s_dram.ap())
    xn_bf = [p.sbuf(f"{pre}_xn{k}", [128, NC], BF16) for k in range(8)]
    h_bf = [p.sbuf(f"{pre}_h{f}", [128, NC], BF16) for f in range(32)]
    ups = [p.psum(f"{pre}_up{i}", [128, NC], F32) for i in range(2)]
    dns = [p.psum(f"{pre}_dn{i}", [128, NC], F32) for i in range(2)]
    xres = p.sbuf(f"{pre}_xres", [128, NC], F32)
    outt = p.sbuf(f"{pre}_outt", [128, NC], F32)
    for c in range(T // NC):
        cs = slice(c * NC, (c + 1) * NC)
        xsrc = [x_rows[k][:, cs] for k in range(8)]
        ln_fm(p, f"{pre}_ln{c}", xsrc, NC, [t[:, :] for t in xn_bf])
        for f in range(32):
            ps = ups[f % 2]
            for k in range(8):
                p.mm(ps[:, :], w1s[:, k * 4096 + f * 128: k * 4096 + (f + 1) * 128],
                     xn_bf[k][:, :], start=(k == 0), stop=(k == 7))
            p.act(h_bf[f][:, :], ps[:, :], AF.Relu, bias=b1s[:, f:f + 1])
        for dt in range(8):
            ps = dns[dt % 2]
            for f in range(32):
                p.mm(ps[:, :], w2s[:, f * 1024 + dt * 128: f * 1024 + (dt + 1) * 128],
                     h_bf[f][:, :], start=(f == 0), stop=(f == 31))
            p.dma("sync", xres[:, :], x_rows[dt][:, cs])
            p.stt(outt[:, :], ps[:, :], b2s[:, dt:dt + 1], xres[:, :],
                  ALU.add, ALU.add)
            p.dma("gpsimd", out_rows[dt][:, cs], outt[:, :])
    p.free(w1s, w2s, b1s, b2s, xres, outt, *xn_bf, *h_bf, *ups, *dns)


def emit_cast_dram(p, pre, src, dst, rows, cols, qs=("sync", "gpsimd")):
    colmax = min(cols, 8192)
    st = p.sbuf(f"{pre}_cst", [128, colmax], F32)
    bt = p.sbuf(f"{pre}_cbt", [128, colmax], BF16)
    for r0 in range(0, rows, 128):
        rw = min(128, rows - r0)
        for c0 in range(0, cols, colmax):
            cw = min(colmax, cols - c0)
            p.dma(qs[0], st[0:rw, :cw], src.ap()[r0:r0 + rw, c0:c0 + cw])
            p.copy(bt[0:rw, :cw], st[0:rw, :cw])
            p.dma(qs[1], dst.ap()[r0:r0 + rw, c0:c0 + cw], bt[0:rw, :cw])
    p.free(st, bt)


def load_wblk(p, pre_unused, wT_dram, K, mt, dst, mwidth=128, q="sync"):
    src = wT_dram.ap().rearrange("(kc p) m -> p kc m", p=128)[
        :, :, mt * 128:mt * 128 + mwidth]
    dst3 = dst.rearrange("p (kc m) -> p kc m", m=mwidth)
    p.dma(q, dst3, src)


def emit_l3_ffn(p, pre, x_dram, out_dram, W, S, nsteps=4, dbg_c=False):
    SC = 256
    bias = p.sbuf(f"{pre}_bias", [128, 32], F32)
    inc = p.sbuf(f"{pre}_inc", [128, 8], F32)
    p.dma("sync", bias[:, :], W["bias"].ap())
    p.dma("sync", inc[:, :], W["in_c"].ap())
    for c0 in range(0, S, SC):
        xn_bf = [p.sbuf(f"{pre}_xn{c0}_{k}", [128, 4 * SC], BF16) for k in range(8)]
        for t in range(4):
            xs = [x_dram.ap()[k * 128:(k + 1) * 128, t * S + c0:t * S + c0 + SC]
                  for k in range(8)]
            ln_fm(p, f"{pre}_ln{c0}_{t}", xs, SC,
                  [xn_bf[k][:, t * SC:(t + 1) * SC] for k in range(8)])
        cf = [p.sbuf(f"{pre}_c{c0}_{k}", [128, SC], F32) for k in range(8)]
        cb = [p.sbuf(f"{pre}_cb{c0}_{k}", [128, SC], BF16) for k in range(8)]
        onesf = p.sbuf(f"{pre}_onesf{c0}", [128, SC], F32)
        p.memset(onesf[:, :], 1.0)
        for k in range(8):
            p.ts(cf[k][:, :], onesf[:, :], inc[:, k:k + 1], op0=ALU.mult)
            p.copy(cb[k][:, :], cf[k][:, :])
        wtile = [p.sbuf(f"{pre}_wt{c0}_{i}", [128, 8 * 128], BF16) for i in range(2)]
        utile = [p.sbuf(f"{pre}_ut{c0}_{i}", [128, 8 * 128], BF16) for i in range(2)]
        ps = [p.psum(f"{pre}_ps{c0}_{i}", [128, SC], F32) for i in range(4)]
        gt = [p.sbuf(f"{pre}_gt{c0}_{k}", [128, SC], F32) for k in range(8)]
        tmp = p.sbuf(f"{pre}_tmp{c0}", [128, SC], F32)
        for t in range(nsteps):
            xn_t = [xn_bf[k][:, t * SC:(t + 1) * SC] for k in range(8)]
            for gi, (wname, uname, fn) in enumerate(
                    (("wc", "uc", AF.Tanh), ("wi", "ui", AF.Sigmoid),
                     ("wf", "uf", AF.Sigmoid))):
                for mt in range(8):
                    wt = wtile[mt % 2]
                    ut = utile[mt % 2]
                    load_wblk(p, pre, W[wname], 1024, mt, wt[:, :], q="sync")
                    load_wblk(p, pre, W[uname], 1024, mt, ut[:, :], q="scalar")
                    bank = ps[mt % 2] if gi != 0 else ps[2 + mt % 2]
                    for kc in range(8):
                        p.mm(bank[:, :], wt[:, kc * 128:(kc + 1) * 128], xn_t[kc],
                             start=(kc == 0), stop=False)
                    for kc in range(8):
                        p.mm(bank[:, :], ut[:, kc * 128:(kc + 1) * 128], cb[kc][:, :],
                             start=False, stop=(kc == 7))
                    gcol = {"wc": 16, "wi": 8, "wf": 0}[wname]
                    if gi == 0:
                        p.act(gt[mt][:, :], bank[:, :], fn,
                              bias=bias[:, gcol + mt:gcol + mt + 1])
                    elif gi == 1:
                        p.act(tmp[:, :], bank[:, :], fn,
                              bias=bias[:, gcol + mt:gcol + mt + 1])
                        p.tt(gt[mt][:, :], gt[mt][:, :], tmp[:, :], ALU.mult)
                    else:
                        p.act(tmp[:, :], bank[:, :], fn,
                              bias=bias[:, gcol + mt:gcol + mt + 1])
                        p.tt(cf[mt][:, :], cf[mt][:, :], tmp[:, :], ALU.mult)
                        p.tt(cf[mt][:, :], cf[mt][:, :], gt[mt][:, :], ALU.add)
            for k in range(8):
                p.copy(cb[k][:, :], cf[k][:, :])
        if dbg_c:
            for k in range(8):
                p.dma("gpsimd", out_dram.ap()[k * 128:(k + 1) * 128, c0:c0 + SC],
                      cf[k][:, :])
            p.free(tmp, *gt, *ps, *wtile, *utile, onesf, *cb, *cf, *xn_bf)
            continue
        xres = p.sbuf(f"{pre}_xres{c0}", [128, SC], F32)
        z = [p.sbuf(f"{pre}_z{c0}_{k}", [128, SC], F32) for k in range(8)]
        for mt in range(8):
            ut = utile[mt % 2]
            load_wblk(p, pre, W["uo"], 1024, mt, ut[:, :], q="scalar")
            bank = ps[mt % 2]
            for kc in range(8):
                p.mm(bank[:, :], ut[:, kc * 128:(kc + 1) * 128], cb[kc][:, :],
                     start=(kc == 0), stop=(kc == 7))
            p.copy(z[mt][:, :], bank[:, :])
        for t in range(4):
            xn_t = [xn_bf[k][:, t * SC:(t + 1) * SC] for k in range(8)]
            for mt in range(8):
                wt = wtile[mt % 2]
                load_wblk(p, pre, W["wo"], 1024, mt, wt[:, :], q="sync")
                bank = ps[2 + mt % 2]
                for kc in range(8):
                    p.mm(bank[:, :], wt[:, kc * 128:(kc + 1) * 128], xn_t[kc],
                         start=(kc == 0), stop=(kc == 7))
                p.stt(tmp[:, :], bank[:, :], bias[:, 24 + mt:24 + mt + 1],
                      z[mt][:, :], ALU.add, ALU.add)
                p.act(tmp[:, :], tmp[:, :], AF.Tanh)
                p.dma("sync", xres[:, :],
                      x_dram.ap()[mt * 128:(mt + 1) * 128, t * S + c0:t * S + c0 + SC])
                p.tt(tmp[:, :], tmp[:, :], xres[:, :], ALU.add)
                p.dma("gpsimd",
                      out_dram.ap()[mt * 128:(mt + 1) * 128, t * S + c0:t * S + c0 + SC],
                      tmp[:, :])
        p.free(xres, tmp, *z, *gt, *ps, *wtile, *utile, onesf, *cb, *cf, *xn_bf)
    p.free(bias, inc)


def load_wrows(p, w_dram, K, cols, dst, q="sync", col0=0, srow=0):
    src = w_dram.ap()[srow:srow + K, col0:col0 + cols].rearrange(
        "(kc p) m -> p kc m", p=128)
    dst3 = dst.rearrange("p (kc m) -> p kc m", m=cols)
    p.dma(q, dst3, src)


def emit_l0_ffn(p, pre, x_rows, out_rows, W, T):
    NC = 512
    nch = T // NC
    oh16 = p.const_onehot16()
    xn = [p.sbuf(f"{pre}_xn{k}", [128, T], BF16) for k in range(8)]
    cw = p.sbuf(f"{pre}_cw", [16, T], F32)
    acc = [p.sbuf(f"{pre}_acc{k}", [128, T], F32) for k in range(8)]
    ln_fm(p, f"{pre}_ln", x_rows, T, [t[:, :] for t in xn],
          xn_f32=[t[:, :] for t in acc])
    wr = p.sbuf(f"{pre}_wr", [128, 8 * 16], F32)
    load_wrows(p, W["Wr"], 1024, 16, wr[:, :])
    lg = p.psum(f"{pre}_lg", [16, NC], F32)
    lgs = p.sbuf(f"{pre}_lgs", [16, NC], F32)
    ltm = p.psum(f"{pre}_ltm", [128, 64], F32)
    prob = p.sbuf(f"{pre}_prob", [128, 16], F32)
    mx8 = p.sbuf(f"{pre}_mx8", [128, 8], F32)
    nmax = p.sbuf(f"{pre}_nmax", [128, 1], F32)
    zz = p.sbuf(f"{pre}_zz", [128, 1], F32)
    cwtm = p.sbuf(f"{pre}_cwtm", [128, 16], F32)
    cwfm = p.psum(f"{pre}_cwfm", [16, 128], F32)
    for c in range(nch):
        cs = slice(c * NC, (c + 1) * NC)
        for k in range(8):
            p.mm(lg[:, :], wr[:, k * 16:(k + 1) * 16], acc[k][:, cs],
                 start=(k == 0), stop=(k == 7))
        p.copy(lgs[:, :], lg[:, :], engine="scalar")
        for tt4 in range(NC // 128):
            p.transpose(ltm[:, tt4 * 16:(tt4 + 1) * 16],
                        lgs[:, tt4 * 128:(tt4 + 1) * 128])
        for tt4 in range(NC // 128):
            tcol = slice(tt4 * 16, (tt4 + 1) * 16)
            p.red(nmax[:, :], ltm[:, tcol], ALU.max, negate=True)
            p.act(prob[:, :], ltm[:, tcol], AF.Exp, bias=nmax[:, 0:1],
                  accum_out=zz[:, 0:1])
            p.recip(zz[:, :], zz[:, :])
            p.ts(prob[:, :], prob[:, :], zz[:, 0:1], op0=ALU.mult)
            p.max8(mx8[:, :], prob[:, :])
            p.ts(cwtm[:, :], prob[:, :], mx8[:, 1:2], op0=ALU.is_ge)
            p.tt(cwtm[:, :], cwtm[:, :], prob[:, :], ALU.mult)
            p.transpose(cwfm[:, :], cwtm[:, :])
            p.copy(cw[:, c * NC + tt4 * 128:c * NC + (tt4 + 1) * 128],
                   cwfm[0:16, 0:128], engine="scalar")
    p.free(lg, lgs, ltm, prob, mx8, nmax, zz, cwtm, cwfm, wr)
    wg = p.sbuf(f"{pre}_wg", [128, 8192], BF16)
    wu = p.sbuf(f"{pre}_wu", [128, 8192], BF16)
    wd = p.sbuf(f"{pre}_wd", [128, 8192], BF16)
    gated = p.sbuf(f"{pre}_gated", [128, 8 * NC], BF16)
    ga2 = p.sbuf(f"{pre}_ga2", [128, 8 * NC], BF16)
    sg = p.sbuf(f"{pre}_sg", [128, NC], F32)
    psA = p.psum(f"{pre}_psA", [128, NC], F32)
    psB = p.psum(f"{pre}_psB", [128, NC], F32)
    psC = p.psum(f"{pre}_psC", [128, NC], F32)
    psW = p.psum(f"{pre}_psW", [128, NC], F32)
    load_wrows(p, W["Sg"], 1024, 1024, wg[:, :], q="sync")
    load_wrows(p, W["Su"], 1024, 1024, wu[:, :], q="scalar")
    load_wrows(p, W["Sd"], 1024, 1024, wd[:, :], q="sync")
    for c in range(nch):
        cs = slice(c * NC, (c + 1) * NC)
        for mt in range(8):
            for k in range(8):
                p.mm(psA[:, :], wg[:, k * 1024 + mt * 128:k * 1024 + (mt + 1) * 128],
                     xn[k][:, cs], start=(k == 0), stop=(k == 7))
            for k in range(8):
                p.mm(psB[:, :], wu[:, k * 1024 + mt * 128:k * 1024 + (mt + 1) * 128],
                     xn[k][:, cs], start=(k == 0), stop=(k == 7))
            p.act(sg[:, :], psA[:, :], AF.Silu)
            p.tt(gated[:, mt * NC:(mt + 1) * NC], sg[:, :], psB[:, :], ALU.mult)
        for dt in range(8):
            for f in range(8):
                p.mm(psC[:, :], wd[:, f * 1024 + dt * 128:f * 1024 + (dt + 1) * 128],
                     gated[:, f * NC:(f + 1) * NC], start=(f == 0), stop=(f == 7))
            p.copy(acc[dt][:, cs], psC[:, :])
    for g in range(8):
        e0, e1 = 2 * g, 2 * g + 1
        for ei, e in enumerate((e0, e1)):
            load_wrows(p, W["Wg"], 1024, 512,
                       wg[:, ei * 4096:(ei + 1) * 4096], q="sync", srow=e * 1024)
            load_wrows(p, W["Wu"], 1024, 512,
                       wu[:, ei * 4096:(ei + 1) * 4096], q="scalar", srow=e * 1024)
            load_wrows(p, W["Wd"], 512, 1024,
                       wd[:, ei * 4096:(ei + 1) * 4096], q="sync", srow=e * 512)
        for c in range(nch):
            cs = slice(c * NC, (c + 1) * NC)
            for ei, e in enumerate((e0, e1)):
                woff = 0 if ei == 0 else 4096
                p.mm(psW[:, :], oh16[:, e * 128:(e + 1) * 128], cw[:, cs],
                     tag="cwbc")
                for ft in range(4):
                    for k in range(8):
                        p.mm(psA[:, :],
                             wg[:, woff + k * 512 + ft * 128:woff + k * 512 + (ft + 1) * 128],
                             xn[k][:, cs], start=(k == 0), stop=(k == 7))
                    for k in range(8):
                        p.mm(psB[:, :],
                             wu[:, woff + k * 512 + ft * 128:woff + k * 512 + (ft + 1) * 128],
                             xn[k][:, cs], start=(k == 0), stop=(k == 7))
                    p.act(sg[:, :], psA[:, :], AF.Silu)
                    p.tt(sg[:, :], sg[:, :], psB[:, :], ALU.mult)
                    gdst = gated if ei == 0 else ga2
                    p.tt(gdst[:, ft * NC:(ft + 1) * NC], sg[:, :], psW[:, :],
                         ALU.mult)
            for dt in range(8):
                for ei in range(2):
                    woff = 0 if ei == 0 else 4096
                    gsrc = gated if ei == 0 else ga2
                    for f in range(4):
                        p.mm(psC[:, :],
                             wd[:, woff + f * 1024 + dt * 128:woff + f * 1024 + (dt + 1) * 128],
                             gsrc[:, f * NC:(f + 1) * NC],
                             start=(ei == 0 and f == 0), stop=(ei == 1 and f == 3))
                p.tt(acc[dt][:, cs], acc[dt][:, cs], psC[:, :], ALU.add)
    xres = p.sbuf(f"{pre}_xres", [128, NC], F32)
    outt = p.sbuf(f"{pre}_outt", [128, NC], F32)
    for c in range(nch):
        cs = slice(c * NC, (c + 1) * NC)
        for dt in range(8):
            p.dma("sync", xres[:, :], x_rows[dt][:, cs])
            p.tt(outt[:, :], acc[dt][:, cs], xres[:, :], ALU.add)
            p.dma("gpsimd", out_rows[dt][:, cs], outt[:, :])
    p.free(xres, outt, wg, wu, wd, gated, ga2, sg, psA, psB, psC, psW,
           cw, *acc, *xn)


def emit_l2_ffn(p, pre, x_rows, out_rows, W, T):
    NC = 512
    nch = T // NC
    HT = 11
    oh16 = p.const_onehot16()
    ones_f1 = p.const_ones(F32, 1, 128)
    xnd = p.dram(f"{pre}_xnd", [1024, T], BF16)
    rw = p.sbuf(f"{pre}_rw", [16, T], F32)
    rwb = p.sbuf(f"{pre}_rwb", [16, T], BF16)
    acc = [p.sbuf(f"{pre}_acc{k}", [128, T], F32) for k in range(8)]
    xn = [p.sbuf(f"{pre}_xn{k}", [128, T], BF16) for k in range(8)]
    ln_fm(p, f"{pre}_ln", x_rows, T, [t[:, :] for t in xn],
          xn_f32=[t[:, :] for t in acc])
    for k in range(8):
        p.dma("gpsimd", xnd.ap()[k * 128:(k + 1) * 128, :], xn[k][:, :])
    gw = p.sbuf(f"{pre}_gw", [128, 8 * 8], F32)
    load_wrows(p, W["gate_wT"], 1024, 8, gw[:, :])
    gb = p.sbuf(f"{pre}_gb", [1, 8], F32)
    p.dma("sync", gb[:, :], W["gate_b"].ap())
    gbps = p.psum(f"{pre}_gbps", [128, 8], F32)
    p.mm(gbps[:, :], ones_f1[:, :], gb[:, :], tag="gbbc")
    gbbc = p.sbuf(f"{pre}_gbbc", [128, 8], F32)
    p.copy(gbbc[:, :], gbps[:, :])
    lg = p.psum(f"{pre}_lg", [16, NC], F32)
    lgs = p.sbuf(f"{pre}_lgs", [16, NC], F32)
    ltm = p.psum(f"{pre}_ltm", [128, 32], F32)
    prob = p.sbuf(f"{pre}_prob", [128, 8], F32)
    biased = p.sbuf(f"{pre}_biased", [128, 8], F32)
    mx8 = p.sbuf(f"{pre}_mx8", [128, 8], F32)
    nmax = p.sbuf(f"{pre}_nmax", [128, 1], F32)
    zz = p.sbuf(f"{pre}_zz", [128, 1], F32)
    rwtm = p.sbuf(f"{pre}_rwtm", [128, 8], F32)
    rwfm = p.psum(f"{pre}_rwfm", [16, 128], F32)
    for c in range(nch):
        cs = slice(c * NC, (c + 1) * NC)
        for k in range(8):
            p.mm(lg[0:8, :], gw[:, k * 8:(k + 1) * 8], acc[k][:, cs],
                 start=(k == 0), stop=(k == 7))
        p.copy(lgs[:, :], lg[:, :], engine="scalar")
        for tt4 in range(NC // 128):
            p.transpose(ltm[:, tt4 * 8:(tt4 + 1) * 8],
                        lgs[0:8, tt4 * 128:(tt4 + 1) * 128])
        for tt4 in range(NC // 128):
            tcol = slice(tt4 * 8, (tt4 + 1) * 8)
            p.red(nmax[:, :], ltm[:, tcol], ALU.max, negate=True)
            p.act(prob[:, :], ltm[:, tcol], AF.Exp, bias=nmax[:, 0:1],
                  accum_out=zz[:, 0:1])
            p.recip(zz[:, :], zz[:, :])
            p.ts(prob[:, :], prob[:, :], zz[:, 0:1], op0=ALU.mult)
            p.tt(biased[:, :], prob[:, :], gbbc[:, :], ALU.add)
            p.max8(mx8[:, :], biased[:, :])
            p.ts(rwtm[:, :], biased[:, :], mx8[:, 1:2], op0=ALU.is_ge)
            p.tt(rwtm[:, :], rwtm[:, :], prob[:, :], ALU.mult)
            p.transpose(rwfm[0:8, :], rwtm[:, :])
            p.copy(rw[0:8, c * NC + tt4 * 128:c * NC + (tt4 + 1) * 128],
                   rwfm[0:8, 0:128], engine="scalar")
    p.copy(rwb[:, :], rw[:, :])
    p.free(lg, lgs, ltm, prob, biased, mx8, nmax, zz, rwtm, rwfm, gw, gb,
           gbps, gbbc, *xn)
    b13 = p.sbuf(f"{pre}_b13", [128, 8 * 22], F32)
    sb13 = p.sbuf(f"{pre}_sb13", [128, 22], F32)
    sb2 = p.sbuf(f"{pre}_sb2", [128, 8], F32)
    b2f = p.sbuf(f"{pre}_b2f", [8, 1024], F32)
    b2w = p.sbuf(f"{pre}_b2w", [8, 1024], BF16)
    p.dma("sync", b13[:, :], W["b13"].ap())
    p.dma("sync", sb13[:, :], W["sb13"].ap())
    p.dma("sync", sb2[:, :], W["sb2"].ap())
    p.dma("sync", b2f[:, :], W["b2"].ap())
    p.copy(b2w[:, :], b2f[:, :])
    w1s = p.sbuf(f"{pre}_w1s", [128, 8 * 1408], BF16)
    w3s = p.sbuf(f"{pre}_w3s", [128, 8 * 1408], BF16)
    w2s = p.sbuf(f"{pre}_w2s", [128, 11 * 1024], BF16)
    gated = p.sbuf(f"{pre}_gated", [128, HT * NC], BF16)
    xnc = p.sbuf(f"{pre}_xnc", [128, 8 * NC], BF16)
    t1 = p.sbuf(f"{pre}_t1", [128, NC], F32)
    psA = p.psum(f"{pre}_psA", [128, NC], F32)
    psB = p.psum(f"{pre}_psB", [128, NC], F32)
    psC = p.psum(f"{pre}_psC", [128, NC], F32)
    psW = p.psum(f"{pre}_psW", [128, NC], F32)
    for e in range(9):
        shared = (e == 8)
        if shared:
            load_wrows(p, W["sw1"], 1024, 1408, w1s[:, :], q="sync")
            load_wrows(p, W["sw3"], 1024, 1408, w3s[:, :], q="scalar")
            load_wrows(p, W["sw2"], 1408, 1024, w2s[:, :], q="sync")
            bc1 = sb13
            boff = 0
        else:
            load_wrows(p, W["w1"], 1024, 1408, w1s[:, :], q="sync", srow=e * 1024)
            load_wrows(p, W["w3"], 1024, 1408, w3s[:, :], q="scalar", srow=e * 1024)
            load_wrows(p, W["w2"], 1408, 1024, w2s[:, :], q="sync", srow=e * 1408)
            bc1 = b13
            boff = e * 22
        for c in range(nch):
            cs = slice(c * NC, (c + 1) * NC)
            load_wrows(p, xnd, 1024, NC, xnc[:, :], q="gpsimd", col0=c * NC)
            if not shared:
                p.mm(psW[:, :], oh16[:, e * 128:(e + 1) * 128], rw[:, cs],
                     tag="rwbc")
            for ft in range(HT):
                for k in range(8):
                    p.mm(psA[:, :],
                         w1s[:, k * 1408 + ft * 128:k * 1408 + (ft + 1) * 128],
                         xnc[:, k * NC:(k + 1) * NC], start=(k == 0), stop=(k == 7))
                for k in range(8):
                    p.mm(psB[:, :],
                         w3s[:, k * 1408 + ft * 128:k * 1408 + (ft + 1) * 128],
                         xnc[:, k * NC:(k + 1) * NC], start=(k == 0), stop=(k == 7))
                p.ts(t1[:, :], psA[:, :], bc1[:, boff + ft:boff + ft + 1],
                     op0=ALU.add)
                p.stt(t1[:, :], psB[:, :], bc1[:, boff + 11 + ft:boff + 11 + ft + 1],
                      t1[:, :], ALU.add, ALU.mult)
                p.act(t1[:, :], t1[:, :], AF.Silu)
                if shared:
                    p.copy(gated[:, ft * NC:(ft + 1) * NC], t1[:, :])
                else:
                    p.tt(gated[:, ft * NC:(ft + 1) * NC], t1[:, :], psW[:, :],
                         ALU.mult)
            for dt in range(8):
                for f in range(HT):
                    p.mm(psC[:, :],
                         w2s[:, f * 1024 + dt * 128:f * 1024 + (dt + 1) * 128],
                         gated[:, f * NC:(f + 1) * NC],
                         start=(f == 0), stop=(f == HT - 1 and e != 0))
                if e == 0:
                    p.mm(psC[:, :], b2w[:, dt * 128:(dt + 1) * 128], rwb[0:8, cs],
                         start=False, stop=True)
                if e == 0:
                    p.copy(acc[dt][:, cs], psC[:, :])
                else:
                    p.tt(acc[dt][:, cs], acc[dt][:, cs], psC[:, :], ALU.add)
    xres = p.sbuf(f"{pre}_xres", [128, NC], F32)
    outt = p.sbuf(f"{pre}_outt", [128, NC], F32)
    for c in range(nch):
        cs = slice(c * NC, (c + 1) * NC)
        for dt in range(8):
            p.dma("sync", xres[:, :], x_rows[dt][:, cs])
            p.stt(outt[:, :], acc[dt][:, cs], sb2[:, dt:dt + 1], xres[:, :],
                  ALU.add, ALU.add)
            p.dma("gpsimd", out_rows[dt][:, cs], outt[:, :])
    p.free(xres, outt, w1s, w3s, w2s, gated, xnc, t1, psA, psB, psC, psW,
           b13, sb13, sb2, b2f, b2w, rw, rwb, *acc)


def emit_fm_proj_residual(p, pre, ctx_rows, x_rows, out_rows, wT, bias_cols, T,
                          kc_n=8, extra_cb=None):
    NC = 512
    wt = [p.sbuf(f"{pre}_wt{i}", [128, kc_n * 128], BF16) for i in range(2)]
    ct = p.sbuf(f"{pre}_ct", [128, kc_n * NC], BF16)
    ps = [p.psum(f"{pre}_ps{i}", [128, NC], F32) for i in range(2)]
    xres = p.sbuf(f"{pre}_xres", [128, NC], F32)
    outt = p.sbuf(f"{pre}_outt", [128, NC], F32)
    for c in range(T // NC):
        cs = slice(c * NC, (c + 1) * NC)
        ct3 = ct[:, :].rearrange("p (kc m) -> p kc m", m=NC)
        for kc in range(kc_n):
            p.dma("gpsimd", ct3[:, kc, :], ctx_rows[kc][:, cs])
        for mt in range(8):
            w = wt[mt % 2]
            load_wblk(p, pre, wT, kc_n * 128, mt, w[:, :], q="sync")
            bank = ps[mt % 2]
            for kc in range(kc_n):
                p.mm(bank[:, :], w[:, kc * 128:(kc + 1) * 128],
                     ct[:, kc * NC:(kc + 1) * NC],
                     start=(kc == 0), stop=(kc == kc_n - 1))
            p.dma("sync", xres[:, :], x_rows[mt][:, cs])
            if bias_cols is not None:
                p.stt(outt[:, :], bank[:, :], bias_cols[:, mt:mt + 1],
                      xres[:, :], ALU.add, ALU.add)
            else:
                p.tt(outt[:, :], bank[:, :], xres[:, :], ALU.add)
            if extra_cb is not None:
                extra_cb(outt, mt, c)
            p.dma("gpsimd", out_rows[mt][:, cs], outt[:, :])
    p.free(*wt, ct, *ps, xres, outt)


def _elu1_fm(p, pre, ps_ap, bias_col, out_bf, t1, t2):
    p.ts(t1, ps_ap, bias_col, 0.0, op0=ALU.add, op1=ALU.min)
    p.act(t1, t1, AF.Exp)
    p.ts(t2, ps_ap, bias_col, 0.0, op0=ALU.add, op1=ALU.max)
    p.tt(out_bf, t1, t2, ALU.add)


def emit_l3_attn_a_ctx(p, pre, x_fm, ctx_out, W, S=4096, NH=8):
    CH = 512
    HD = 64
    ones_f1 = p.const_ones(F32, 1, 128)
    nhp = NH // 2
    qb = p.sbuf(f"{pre}_qb", [128, nhp], F32)
    kb = p.sbuf(f"{pre}_kb", [128, nhp], F32)
    vbr = p.sbuf(f"{pre}_vbr", [1, NH * HD], F32)
    t1 = p.sbuf(f"{pre}_t1", [128, CH], F32)
    t2 = p.sbuf(f"{pre}_t2", [128, CH], F32)
    wqs = p.sbuf(f"{pre}_wqs", [128, 8 * NH * HD], BF16)
    wks = p.sbuf(f"{pre}_wks", [128, 8 * NH * HD], BF16)
    wvs = p.sbuf(f"{pre}_wvs", [128, 8 * NH * HD], BF16)
    xnc = p.sbuf(f"{pre}_xnc", [128, 8 * CH], BF16)
    vtmp = p.sbuf(f"{pre}_vtmp", [128, NH * HD], F32)
    qf = [p.sbuf(f"{pre}_qf{j}", [128, S], BF16) for j in range(nhp)]
    kf = [p.sbuf(f"{pre}_kf{j}", [128, S], BF16) for j in range(nhp)]
    kvt = p.sbuf(f"{pre}_kvt", [128, (S // 128) * NH * 129], BF16)
    kvt4 = kvt[:, :].rearrange("p (tt h c) -> p tt h c", h=NH, c=129)
    pst = p.psum(f"{pre}_pst", [128, 128], BF16)
    p.dma("sync", qb[:, :], W["qb"].ap())
    p.dma("sync", kb[:, :], W["kb"].ap())
    p.dma("sync", vbr[:, :], W["vb_row"].ap())
    p.memset(kvt4[:, :, :, 128:129], 1.0)
    xnd = p.dram(f"{pre}_xnd", [1024, S], BF16)
    ln_fm(p, f"{pre}_ln", x_fm, S, xn_bf=None, xn_dram=xnd)
    psq = p.psum(f"{pre}_psq", [128, CH], F32)
    psk = p.psum(f"{pre}_psk", [128, CH], F32)
    psv = p.psum(f"{pre}_psv", [128, NH * HD], F32)
    vbbc = p.psum(f"{pre}_vbbc", [128, NH * HD], F32)
    load_wrows(p, W["qkvw"], 1024, NH * HD, wqs[:, :], q="sync", col0=0)
    load_wrows(p, W["qkvw"], 1024, NH * HD, wks[:, :], q="scalar",
               col0=NH * HD)
    load_wrows(p, W["qkvw"], 1024, NH * HD, wvs[:, :], q="sync",
               col0=2 * NH * HD)
    p.mm(vbbc[:, :], ones_f1[:, :], vbr[:, 0:NH * HD], tag="vbbc")
    vbbs = p.sbuf(f"{pre}_vbbs", [128, NH * HD], F32)
    p.copy(vbbs[:, :], vbbc[:, :])
    SW = NH * HD
    for c in range(S // CH):
        cs = slice(c * CH, (c + 1) * CH)
        load_wrows(p, xnd, 1024, CH, xnc[:, :], q="gpsimd", col0=c * CH)
        for tt in range(CH // 128):
            for k in range(8):
                p.mm(psv[:, :], xnc[:, k * CH + tt * 128:k * CH + (tt + 1) * 128],
                     wvs[:, k * SW:(k + 1) * SW],
                     start=(k == 0), stop=(k == 7))
            p.tt(vtmp[:, :], psv[:, :], vbbs[:, :], ALU.add)
            gtt = c * (CH // 128) + tt
            for h in range(NH):
                p.copy(kvt4[:, gtt, h, 64:128], vtmp[:, h * HD:(h + 1) * HD])
        for j in range(nhp):
            for k in range(8):
                p.mm(psq[:, :], wqs[:, k * SW + j * 128:k * SW + (j + 1) * 128],
                     xnc[:, k * CH:(k + 1) * CH], start=(k == 0), stop=(k == 7))
            _elu1_fm(p, pre, psq[:, :], qb[:, j:j + 1], qf[j][:, cs], t1[:, :],
                     t2[:, :])
            for k in range(8):
                p.mm(psk[:, :], wks[:, k * SW + j * 128:k * SW + (j + 1) * 128],
                     xnc[:, k * CH:(k + 1) * CH], start=(k == 0), stop=(k == 7))
            _elu1_fm(p, pre, psk[:, :], kb[:, j:j + 1], kf[j][:, cs], t1[:, :],
                     t2[:, :])
    p.free(vbbs, vbbc, psv, psk, psq)
    for j in range(nhp):
        for tt in range(S // 128):
            p.transpose(pst[:, :], kf[j][:, tt * 128:(tt + 1) * 128])
            p.copy(kvt4[:, tt, 2 * j, 0:64], pst[:, 0:64], tag="kcp")
            p.copy(kvt4[:, tt, 2 * j + 1, 0:64], pst[:, 64:128], tag="kcp2")
    amask = p.sbuf(f"{pre}_amask", [128, 4 * CH], BF16)
    am4 = amask[:, :].rearrange("p (ti c) -> p ti c", c=CH)
    rowi = p.sbuf(f"{pre}_rowi", [128, CH], F32)
    coli = p.sbuf(f"{pre}_coli", [128, CH], F32)
    p.iota(coli[:, :], [[1, CH]], base=0, channel_multiplier=0)
    for ti in range(4):
        p.iota(rowi[:, :], [[0, CH]], base=ti * 128, channel_multiplier=1)
        p.tt(am4[:, ti, :], rowi[:, :], coli[:, :], ALU.is_le)
    asb = p.sbuf(f"{pre}_asb", [128, 4 * CH], BF16)
    Ssb = p.sbuf(f"{pre}_Ssb", [128, 65], BF16)
    den = p.sbuf(f"{pre}_den", [1, CH], F32)
    nums = p.sbuf(f"{pre}_nums", [64, CH], F32)
    ctxt = p.sbuf(f"{pre}_ctxt", [64, CH], BF16)
    psa = [p.psum(f"{pre}_psa{i}", [128, CH], F32) for i in range(2)]
    pso = p.psum(f"{pre}_pso", [128, CH], F32)
    psS = p.psum(f"{pre}_psS", [128, 65], F32)
    dbc = p.psum(f"{pre}_dbc", [64, CH], F32)
    nch = S // CH
    nt = CH // 128
    for h in range(NH):
        j, half = h // 2, h % 2
        hrow = slice(64 * half, 64 * half + 64)
        for c in range(nch):
            cs = slice(c * CH, (c + 1) * CH)
            for ti in range(nt):
                taus = slice(c * CH + ti * 128, c * CH + (ti + 1) * 128)
                p.mm(psa[ti % 2][:, :], kf[j][hrow, taus], qf[j][hrow, cs],
                     tag="at")
                p.tt(asb[:, ti * CH:(ti + 1) * CH], psa[ti % 2][:, :],
                     am4[:, ti, :], ALU.mult)
            for ti in range(nt):
                tile_i = c * nt + ti
                p.mm(pso[0:65, :], kvt4[:, tile_i, h, 64:129],
                     asb[:, ti * CH:(ti + 1) * CH],
                     start=(ti == 0), stop=(c == 0 and ti == nt - 1))
            if c > 0:
                p.mm(pso[0:65, :], Ssb[hrow, :], qf[j][hrow, cs],
                     start=False, stop=True)
            for ti in range(nt):
                tile_i = c * nt + ti
                p.mm(psS[hrow, :], kvt4[:, tile_i, h, 0:64],
                     kvt4[:, tile_i, h, 64:129],
                     start=(c == 0 and ti == 0),
                     stop=(c == nch - 1 and ti == nt - 1))
            if c < nch - 1:
                p.copy(Ssb[hrow, :], psS[hrow, :])
            p.ts(den[:, :], pso[64:65, :], 1e-6, op0=ALU.add)
            p.recip(den[:, :], den[:, :])
            p.copy(nums[:, :], pso[0:64, :], engine="scalar")
            p.mm(dbc[:, :], ones_f1[:, 0:64], den[:, :], tag="dbc")
            p.tt(ctxt[:, :], nums[:, :], dbc[:, :], ALU.mult)
            p.dma("gpsimd", ctx_out.ap()[h * 64:(h + 1) * 64, cs], ctxt[:, :])
    p.free(amask, rowi, coli, asb, Ssb, den, nums, ctxt, *psa, pso, psS, dbc)
    p.free(kvt, *kf, *qf, vtmp, xnc, wvs, wks, wqs, t2, t1, vbr, kb, qb, pst)


def emit_mamba_pre(p, pre, xh_rows, outs, W, T, XH=512, stages=2):
    u_bfd = p.dram(f"{pre}_ubfd", [2048, T], BF16)
    TH = T + XH
    NC = 512
    xnt = [p.sbuf(f"{pre}_xnt{k}", [128, TH], BF16) for k in range(8)]
    ln_fm(p, f"{pre}_ln", xh_rows, TH, xn_bf=[t[:, :] for t in xnt])
    convw = p.sbuf(f"{pre}_convw", [128, 16 * 4], F32)
    convb = p.sbuf(f"{pre}_convb", [128, 16], F32)
    dtb = p.sbuf(f"{pre}_dtb", [128, 16], F32)
    dtw = p.sbuf(f"{pre}_dtw", [64, 2048], BF16)
    p.dma("sync", convw[:, :], W["convw"].ap())
    p.dma("sync", convb[:, :], W["convb"].ap())
    p.dma("sync", dtb[:, :], W["dtb"].ap())
    p.dma("sync", dtw[:, :], W["dtprojT"].ap())
    wblk = p.sbuf(f"{pre}_wblk", [128, 8 * 128], BF16)
    xitile = p.sbuf(f"{pre}_xi", [128, TH], F32)
    acc1 = p.sbuf(f"{pre}_a1", [128, T], F32)
    acc2 = p.sbuf(f"{pre}_a2", [128, T], F32)
    ubf = p.sbuf(f"{pre}_ubf", [128, T], BF16)
    psi = p.psum(f"{pre}_psi", [128, NC], F32)
    for mt in range(32):
        load_wblk(p, pre, W["inprojT"], 1024, mt, wblk[:, :], q="sync")
        for c in range(TH // NC):
            cs = slice(c * NC, (c + 1) * NC)
            for k in range(8):
                p.mm(psi[:, :], wblk[:, k * 128:(k + 1) * 128],
                     xnt[k][:, cs], start=(k == 0), stop=(k == 7))
            if mt < 16:
                p.copy(xitile[:, cs], psi[:, :])
            else:
                if c == 0:
                    continue
                p.act(acc1[:, slice((c - 1) * NC, c * NC)], psi[:, :], AF.Silu)
        if mt < 16:
            ct = mt
            sh = [xitile[:, XH - 3 + j:XH - 3 + j + T] for j in range(4)]
            p.ts(acc1[:, :], sh[0], convw[:, ct * 4:ct * 4 + 1], op0=ALU.mult)
            p.stt(acc2[:, :], sh[1], convw[:, ct * 4 + 1:ct * 4 + 2], acc1[:, :],
                  ALU.mult, ALU.add)
            p.stt(acc1[:, :], sh[2], convw[:, ct * 4 + 2:ct * 4 + 3], acc2[:, :],
                  ALU.mult, ALU.add)
            p.stt(acc2[:, :], sh[3], convw[:, ct * 4 + 3:ct * 4 + 4], acc1[:, :],
                  ALU.mult, ALU.add)
            p.act(acc1[:, :], acc2[:, :], AF.Silu, bias=convb[:, ct:ct + 1])
            p.dma("gpsimd", outs["u_f32"].ap()[ct * 128:(ct + 1) * 128, :],
                  acc1[:, :])
            p.copy(ubf[:, :], acc1[:, :])
            p.dma("gpsimd", u_bfd.ap()[ct * 128:(ct + 1) * 128, :],
                  ubf[:, :])
        else:
            p.dma("gpsimd", outs["zs"].ap()[(mt - 16) * 128:(mt - 15) * 128, :],
                  acc1[:, :])
    p.free(wblk, xitile, acc1, acc2, ubf, psi)
    if stages <= 1:
        p.free(convw, convb, dtb, dtw, *xnt)
        return
    xpw = p.sbuf(f"{pre}_xpw", [128, 16 * 96], BF16)
    load_wrows(p, W["xprojT"], 2048, 96, xpw[:, :], q="sync")
    ubc = p.sbuf(f"{pre}_ubc", [128, 16 * NC], BF16)
    xd_f = p.sbuf(f"{pre}_xdf", [96, NC], F32)
    xd_b = p.sbuf(f"{pre}_xdb", [96, NC], BF16)
    dsb = p.sbuf(f"{pre}_dsb", [128, NC], F32)
    psxd = p.psum(f"{pre}_psxd", [96, NC], F32)
    psd = p.psum(f"{pre}_psd", [128, NC], F32)
    for c in range(T // NC):
        cs = slice(c * NC, (c + 1) * NC)
        load_wrows(p, u_bfd, 2048, NC, ubc[:, :], q="sync", col0=c * NC)
        if stages < 1.15:
            continue
        for k in range(16):
            p.mm(psxd[:, :], xpw[:, k * 96:(k + 1) * 96],
                 ubc[:, k * NC:(k + 1) * NC], start=(k == 0), stop=(k == 15))
        if stages < 1.2:
            continue
        p.copy(xd_f[:, :], psxd[:, :])
        p.copy(xd_b[:, :], psxd[:, :], engine="scalar")
        if stages >= 1.25:
            p.dma("gpsimd", outs["BC"].ap()[:, cs], xd_f[64:96, :])
        if stages < 1.5:
            continue
        for mt in range(16):
            p.mm(psd[:, :], dtw[:, mt * 128:(mt + 1) * 128], xd_b[0:64, :],
                 tag="dmm")
            if stages < 1.75:
                p.copy(dsb[:, :], psd[:, :])
            else:
                p.act(dsb[:, :], psd[:, :], AF.Exp, bias=dtb[:, mt:mt + 1])
                p.act(dsb[:, :], dsb[:, :], AF.Ln, bias=1.0)
            p.dma("gpsimd", outs["delta"].ap()[mt * 128:(mt + 1) * 128, cs],
                  dsb[:, :])
    p.free(xpw, ubc, xd_f, xd_b, dsb, psxd, psd, convw, convb, dtb,
           dtw, *xnt)


def emit_mamba_scan(p, pre, ins, y_out, SB=4096, NB=4, NCT=2, stages=2):
    alog = p.sbuf(f"{pre}_alog", [128, NCT * 16], F32)
    aval = p.sbuf(f"{pre}_aval", [128, NCT * 16], F32)
    p.dma("sync", alog[:, :], ins["alog"].ap())
    p.act(aval[:, :], alog[:, :], AF.Exp)
    p.ts(aval[:, :], aval[:, :], -1.0, op0=ALU.mult)
    dl = [p.sbuf(f"{pre}_dl{i}", [128, SB], F32) for i in range(NCT)]
    du = [p.sbuf(f"{pre}_du{i}", [128, SB], F32) for i in range(NCT)]
    uu = p.sbuf(f"{pre}_uu", [128, SB], F32)
    bbc = p.sbuf(f"{pre}_bbc", [128, SB], F32)
    cbc = p.sbuf(f"{pre}_cbc", [128, SB], F32)
    da = p.sbuf(f"{pre}_da", [128, SB], F32)
    dbu = p.sbuf(f"{pre}_dbu", [128, SB], F32)
    hh = p.sbuf(f"{pre}_hh", [128, SB], F32)
    hc = p.sbuf(f"{pre}_hc", [128, SB], F32)

    def bcast_row(dst, dram_t, row, col0):
        base = dram_t.ap()[row:row + 1, col0:col0 + SB]
        src = bass.AP(base.tensor, base.offset, [[0, 128], [1, SB]])
        p.dma("scalar", dst, src, tag="bcast")

    for b in range(NB):
        cs = slice(b * SB, (b + 1) * SB)
        for ct in range(NCT):
            rs = slice(ct * 128, (ct + 1) * 128)
            p.dma("sync", dl[ct][:, :], ins["delta"].ap()[rs, cs])
            p.dma("sync", uu[:, :], ins["u"].ap()[rs, cs])
            p.tt(du[ct][:, :], dl[ct][:, :], uu[:, :], ALU.mult)
        for n in range(16):
            if stages < 1.2:
                continue
            bcast_row(bbc[:, :], ins["BC"], n, b * SB)
            bcast_row(cbc[:, :], ins["BC"], 16 + n, b * SB)
            for ct in range(NCT):
                rs = slice(ct * 128, (ct + 1) * 128)
                if stages < 1.4:
                    continue
                p.act(da[:, :], dl[ct][:, :], AF.Exp,
                      scale=aval[:, ct * 16 + n:ct * 16 + n + 1])
                if stages < 1.6:
                    continue
                p.tt(dbu[:, :], du[ct][:, :], bbc[:, :], ALU.mult)
                p.scan(hh[:, :], da[:, :], dbu[:, :], 0.0, ALU.mult, ALU.add)
                if stages < 1.8:
                    continue
                p.tt(hc[:, :], hh[:, :], cbc[:, :], ALU.mult, engine="gpsimd")
                if stages < 2:
                    continue
                if n == 0:
                    p.dma("gpsimd", y_out.ap()[rs, cs], hc[:, :], tag="yw")
                else:
                    for hseg in range(SB // 2048):
                        seg = slice(b * SB + hseg * 2048, b * SB + (hseg + 1) * 2048)
                        segl = slice(hseg * 2048, (hseg + 1) * 2048)
                        p.op("gpsimd",
                             lambda e, _o=y_out.ap()[rs, seg], _i=hc[:, segl]:
                             e.dma_start(out=_o, in_=_i, accum_op=ALU.add),
                             reads=[hc[:, segl], y_out.ap()[rs, seg]],
                             writes=[y_out.ap()[rs, seg]], inc=16, tag="yacc")
        if stages < 2:
            for ct in range(NCT):
                rs = slice(ct * 128, (ct + 1) * 128)
                p.dma("gpsimd", y_out.ap()[rs, cs], hc[:, :], tag="ywd")
    p.free(alog, aval, *dl, *du, uu, bbc, cbc, da, dbu, hh, hc)


def emit_mamba_post(p, pre, ins, x_rows, out_rows, W, T):
    NC = 512
    Dcol = p.sbuf(f"{pre}_Dcol", [128, 16], F32)
    p.dma("sync", Dcol[:, :], W["Dcol"].ap())
    yt = p.sbuf(f"{pre}_yt", [128, NC], F32)
    ut = p.sbuf(f"{pre}_ut", [128, NC], F32)
    zt = p.sbuf(f"{pre}_zt", [128, NC], F32)
    y2 = p.sbuf(f"{pre}_y2", [128, NC], BF16)
    y2d = p.dram(f"{pre}_y2d", [2048, T], BF16)
    for mt in range(16):
        rs = slice(mt * 128, (mt + 1) * 128)
        for c in range(T // NC):
            cs = slice(c * NC, (c + 1) * NC)
            p.dma("sync", yt[:, :], ins["y"].ap()[rs, cs])
            p.dma("sync", ut[:, :], ins["u_f32"].ap()[rs, cs])
            p.dma("scalar", zt[:, :], ins["zs"].ap()[rs, cs])
            p.stt(yt[:, :], ut[:, :], Dcol[:, mt:mt + 1], yt[:, :],
                  ALU.mult, ALU.add)
            p.tt(y2[:, :], yt[:, :], zt[:, :], ALU.mult)
            p.dma("gpsimd", y2d.ap()[rs, cs], y2[:, :])
    p.free(yt, ut, zt, y2)
    y2_rows = [y2d.ap()[k * 128:(k + 1) * 128, :] for k in range(16)]
    emit_fm_proj_residual(p, f"{pre}_pr", y2_rows, x_rows, out_rows,
                          W["outprojT"], None, T, kc_n=16)
    p.free(Dcol)


def emit_l1_attn_a_ctx(p, pre, x_fm, att_out, W, S=4096, NH=4):
    CH = 512
    TAU = float(2 * np.pi)
    ones_f1 = p.const_ones(F32, 1, 128)
    h0 = W["h0"]
    qkb = p.sbuf(f"{pre}_qkb", [128, 32], F32)
    theta = p.sbuf(f"{pre}_theta", [128, NH], F32)
    slr = p.sbuf(f"{pre}_slr", [1, 8], F32)
    chm1 = p.sbuf(f"{pre}_chm1", [1, 8], F32)
    chcol = p.sbuf(f"{pre}_chcol", [128, NH], F32)
    vbr = p.sbuf(f"{pre}_vbr", [1, NH * 128], F32)
    vbbs = p.sbuf(f"{pre}_vbbs", [128, 128], F32)
    pchc = p.psum(f"{pre}_pchc", [128, NH], F32)
    p.dma("sync", qkb[:, :], W["qkvub"].ap())
    p.dma("sync", theta[:, :], W["theta"].ap())
    p.dma("sync", slr[:, :], W["slr"].ap())
    p.dma("sync", vbr[:, :], W["vbias_row"].ap())
    p.act(chm1[:, :], slr[:, :], AF.Exp)
    p.ts(chm1[:, :], chm1[:, :], -1.0, op0=ALU.add)
    p.mm(pchc[:, :], ones_f1[:, :], chm1[:, h0:h0 + NH], tag="chbc")
    p.copy(chcol[:, :], pchc[:, :])
    xnd = p.dram(f"{pre}_xnd", [1024, S], BF16)
    ln_fm(p, f"{pre}_ln", x_fm, S, xn_bf=None, xn_dram=xnd)
    HS = S // 2
    th = p.sbuf(f"{pre}_th", [128, HS], F32)
    v1 = p.sbuf(f"{pre}_v1", [128, HS], F32)
    vi = p.sbuf(f"{pre}_vi", [128, HS], I32)
    cs_t = p.sbuf(f"{pre}_cs", [128, S], F32)
    sn_t = p.sbuf(f"{pre}_sn", [128, S], F32)
    amask = p.sbuf(f"{pre}_amask", [128, 4 * CH], BF16)
    am4 = amask[:, :].rearrange("p (ti c) -> p ti c", c=CH)
    rowi = p.sbuf(f"{pre}_rowi", [128, CH], F32)
    p.iota(v1[:, 0:CH], [[1, CH]], base=0, channel_multiplier=0)
    for ti in range(4):
        p.iota(rowi[:, :], [[0, CH]], base=ti * 128, channel_multiplier=1)
        p.tt(am4[:, ti, :], rowi[:, :], v1[:, 0:CH], ALU.is_le)
    xnc = p.sbuf(f"{pre}_xnc", [128, 8 * CH], BF16)
    wq = p.sbuf(f"{pre}_wq", [128, 8 * 128], BF16)
    wv = p.sbuf(f"{pre}_wv", [128, 8 * 128], BF16)
    qpre = p.sbuf(f"{pre}_qpre", [128, S], F32)
    kpre = p.sbuf(f"{pre}_kpre", [128, S], F32)
    qf = [p.sbuf(f"{pre}_qf{e}", [128, S], BF16) for e in range(2)]
    kf = [p.sbuf(f"{pre}_kf{e}", [128, S], BF16) for e in range(2)]
    ktm = p.sbuf(f"{pre}_ktm", [128, (S // 128) * 256], BF16)
    ktm3 = ktm[:, :].rearrange("p (tt e) -> p tt e", e=256)
    vtm = p.sbuf(f"{pre}_vtm", [128, (S // 128) * 128], BF16)
    vtm3 = vtm[:, :].rearrange("p (tt d) -> p tt d", d=128)
    vtmp = p.sbuf(f"{pre}_vtmp", [128, 128], F32)
    SFs = [p.sbuf(f"{pre}_SFs{e}", [128, 128], BF16) for e in range(2)]
    asb = p.sbuf(f"{pre}_asb", [128, 4 * CH], BF16)
    cau = p.sbuf(f"{pre}_cau", [128, S], BF16)
    att = p.sbuf(f"{pre}_att", [128, CH], F32)
    psp = p.psum(f"{pre}_psp", [128, CH], F32)
    pst = p.psum(f"{pre}_pst", [128, 128], BF16)
    psv = p.psum(f"{pre}_psv", [128, 128], F32)
    SFe = [p.psum(f"{pre}_SF{e}", [128, 128], F32) for e in range(2)]
    psc = p.psum(f"{pre}_psc", [128, CH], F32)
    psf = p.psum(f"{pre}_psf", [128, CH], F32)
    for h in range(NH):
        for dst, shift in ((sn_t, 0.0), (cs_t, float(np.pi / 2))):
            for hs in range(2):
                sl = slice(hs * HS, (hs + 1) * HS)
                p.iota(th[:, :], [[1, HS]], base=hs * HS, channel_multiplier=0)
                p.ts(th[:, :], th[:, :], theta[:, h:h + 1], shift,
                     op0=ALU.mult, op1=ALU.add)
                p.ts(v1[:, :], th[:, :], 1.0 / TAU, op0=ALU.mult)
                p.copy(vi[:, :], v1[:, :])
                p.copy(v1[:, :], vi[:, :])
                p.stt(th[:, :], v1[:, :], -TAU, th[:, :], ALU.mult, ALU.add)
                p.ts(v1[:, :], th[:, :], float(np.pi), op0=ALU.is_gt)
                p.stt(th[:, :], v1[:, :], -TAU, th[:, :], ALU.mult, ALU.add)
                p.ts(v1[:, :], th[:, :], float(-np.pi), op0=ALU.is_lt)
                p.stt(th[:, :], v1[:, :], TAU, th[:, :], ALU.mult, ALU.add)
                p.act(dst[:, sl], th[:, :], AF.Sin)
        p.mm(psv[:, :], ones_f1[:, :], vbr[:, h * 128:(h + 1) * 128],
             tag="vbbc")
        p.copy(vbbs[:, :], psv[:, :])
        load_wrows(p, W["qkvuT"], 1024, 128, wv[:, :], q="scalar",
                   col0=2048 + (h0 + h) * 128)
        for c in range(S // CH):
            cs = slice(c * CH, (c + 1) * CH)
            load_wrows(p, xnd, 1024, CH, xnc[:, :], q="sync", col0=c * CH)
            for tt in range(CH // 128):
                for k in range(8):
                    p.mm(psv[:, :], xnc[:, k * CH + tt * 128:k * CH + (tt + 1) * 128],
                         wv[:, k * 128:(k + 1) * 128],
                         start=(k == 0), stop=(k == 7))
                gtt = c * (CH // 128) + tt
                p.tt(vtmp[:, :], psv[:, :], vbbs[:, :], ALU.add)
                p.copy(vtm3[:, gtt, :], vtmp[:, :])
            load_wrows(p, W["qkvuT"], 1024, 128, wq[:, :], q="scalar",
                       col0=(h0 + h) * 128)
            for k in range(8):
                p.mm(psp[:, :], wq[:, k * 128:(k + 1) * 128],
                     xnc[:, k * CH:(k + 1) * CH], start=(k == 0), stop=(k == 7))
            p.act(qpre[:, cs], psp[:, :], AF.Silu,
                  bias=qkb[:, h0 + h:h0 + h + 1])
            load_wrows(p, W["qkvuT"], 1024, 128, wq[:, :], q="scalar",
                       col0=1024 + (h0 + h) * 128)
            for k in range(8):
                p.mm(psp[:, :], wq[:, k * 128:(k + 1) * 128],
                     xnc[:, k * CH:(k + 1) * CH], start=(k == 0), stop=(k == 7))
            p.act(kpre[:, cs], psp[:, :], AF.Silu,
                  bias=qkb[:, 8 + h0 + h:8 + h0 + h + 1])
        p.tt(qf[0][:, :], qpre[:, :], cs_t[:, :], ALU.mult)
        p.tt(qf[1][:, :], qpre[:, :], sn_t[:, :], ALU.mult)
        p.tt(kf[0][:, :], kpre[:, :], cs_t[:, :], ALU.mult)
        p.tt(kf[1][:, :], kpre[:, :], sn_t[:, :], ALU.mult)
        for e in range(2):
            for tt in range(S // 128):
                p.transpose(pst[:, :], kf[e][:, tt * 128:(tt + 1) * 128])
                p.copy(ktm3[:, tt, e * 128:(e + 1) * 128], pst[:, :])
        nch = S // CH
        nt = CH // 128
        for c in range(nch):
            cs = slice(c * CH, (c + 1) * CH)
            for ti in range(nt):
                taus = slice(c * CH + ti * 128, c * CH + (ti + 1) * 128)
                for e in range(2):
                    p.mm(psp[:, :], kf[e][:, taus], qf[e][:, cs],
                         start=(e == 0), stop=(e == 1))
                p.tt(asb[:, ti * CH:(ti + 1) * CH], psp[:, :],
                     am4[:, ti, :], ALU.mult)
            for ti in range(nt):
                tile_i = c * nt + ti
                p.mm(psc[:, :], vtm3[:, tile_i, :],
                     asb[:, ti * CH:(ti + 1) * CH],
                     start=(ti == 0), stop=(c == 0 and ti == nt - 1))
            if c > 0:
                for e in range(2):
                    p.mm(psc[:, :], SFs[e][:, :], qf[e][:, cs],
                         start=False, stop=(e == 1))
            p.ts(cau[:, cs], psc[:, :], chcol[:, h:h + 1], op0=ALU.mult)
            for e in range(2):
                for ti in range(nt):
                    tile_i = c * nt + ti
                    p.mm(SFe[e][:, :], ktm3[:, tile_i, e * 128:(e + 1) * 128],
                         vtm3[:, tile_i, :], start=(c == 0 and ti == 0),
                         stop=(c == nch - 1 and ti == nt - 1))
                p.copy(SFs[e][:, :], SFe[e][:, :])
        for c in range(nch):
            cs = slice(c * CH, (c + 1) * CH)
            for e in range(2):
                p.mm(psf[:, :], SFs[e][:, :], qf[e][:, cs],
                     start=(e == 0), stop=(e == 1))
            p.tt(att[:, :], cau[:, cs], psf[:, :], ALU.add)
            p.dma("gpsimd", att_out.ap()[h * 128:(h + 1) * 128, cs], att[:, :])
    p.free(qkb, theta, slr, chm1, chcol, vbr, vbbs, pchc, th, v1, vi,
           cs_t, sn_t, amask, rowi, xnc, wq, wv, qpre, kpre, *qf, *kf, ktm,
           vtm, vtmp, *SFs, asb, cau, att, psp, pst, psv, *SFe, psc, psf)


def emit_l1_attn_a_out(p, pre, att_full_rows, x_rows, out_rows, W, T):
    NC = 512
    ones_bf = p.const_ones(BF16, 128, 1)
    ones_f1 = p.const_ones(F32, 1, 128)
    ubias = p.sbuf(f"{pre}_ub", [128, 8], F32)
    outb = p.sbuf(f"{pre}_outb", [128, 8], F32)
    p.dma("sync", ubias[:, :], W["ubias"].ap())
    p.dma("sync", outb[:, :], W["outb"].ap())
    xn = [p.sbuf(f"{pre}_xn{k}", [128, T], BF16) for k in range(8)]
    ln_fm(p, f"{pre}_ln2", x_rows, T, [t[:, :] for t in xn])
    gd = p.dram(f"{pre}_gd", [1024, T], BF16)
    att_t = [p.sbuf(f"{pre}_at{i}", [128, NC], F32) for i in range(2)]
    attb = p.sbuf(f"{pre}_attb", [128, NC], BF16)
    sq = p.sbuf(f"{pre}_sq", [128, NC], BF16)
    ms = p.sbuf(f"{pre}_ms", [1, NC], F32)
    pss = p.psum(f"{pre}_pss", [1, NC], F32)
    psb = p.psum(f"{pre}_psb", [128, NC], F32)
    psu = p.psum(f"{pre}_psu", [128, NC], F32)
    wu = p.sbuf(f"{pre}_wu", [128, 8 * 128], BF16)
    gt = p.sbuf(f"{pre}_gt", [128, NC], F32)
    for c in range(T // NC):
        cs = slice(c * NC, (c + 1) * NC)
        for k in range(8):
            p.dma("sync", att_t[k % 2][:, :], att_full_rows[k][:, cs])
            p.copy(attb[:, :], att_t[k % 2][:, :])
            p.act(sq[:, :], att_t[k % 2][:, :], AF.Square)
            p.mm(pss[0:1, :], ones_bf[:, :], sq[:, :], start=(k == 0),
                 stop=(k == 7))
        p.ts(ms[:, :], pss[0:1, :], 1.0 / 1024.0, op0=ALU.mult)
        p.act(ms[:, :], ms[:, :], AF.Sqrt, bias=1e-6)
        p.recip(ms[:, :], ms[:, :])
        p.mm(psb[:, :], ones_f1[:, :], ms[:, :], tag="rsbc")
        for mt in range(8):
            load_wrows(p, W["qkvuT"], 1024, 128, wu[:, :], q="scalar",
                       col0=3072 + mt * 128)
            for k in range(8):
                p.mm(psu[:, :], wu[:, k * 128:(k + 1) * 128], xn[k][:, cs],
                     start=(k == 0), stop=(k == 7))
            p.dma("sync", att_t[0][:, :], att_full_rows[mt][:, cs])
            p.tt(gt[:, :], att_t[0][:, :], psb[:, :], ALU.mult)
            p.stt(gt[:, :], psu[:, :], ubias[:, mt:mt + 1], gt[:, :],
                  ALU.add, ALU.mult)
            p.copy(attb[:, :], gt[:, :])
            p.dma("gpsimd", gd.ap()[mt * 128:(mt + 1) * 128, cs], attb[:, :])
    p.free(*xn, *att_t, attb, sq, ms, pss, psb, psu, wu, gt)
    gd_rows = [gd.ap()[k * 128:(k + 1) * 128, :] for k in range(8)]
    emit_fm_proj_residual(p, f"{pre}_pr", gd_rows, x_rows, out_rows,
                          W["outwT"], outb[:, :], T)
    p.free(outb, ubias)


def emit_l0_attn_a_ctx(p, pre, x_fm, ctx_out, W, S=4096, NH=8):
    QC = 512
    ones_f1 = p.const_ones(F32, 1, 128)
    qb = p.sbuf(f"{pre}_qb", [128, NH // 2], F32)
    kb = p.sbuf(f"{pre}_kb", [128, NH // 2], F32)
    vbr = p.sbuf(f"{pre}_vbr", [1, NH * 64], F32)
    p.dma("sync", qb[:, :], W["qb"].ap())
    p.dma("sync", kb[:, :], W["kb"].ap())
    p.dma("sync", vbr[:, :], W["vb_row"].ap())
    xnd = p.dram(f"{pre}_xnd", [1024, S], BF16)
    ln_fm(p, f"{pre}_ln", x_fm, S, xn_bf=None, xn_dram=xnd)
    xnc = p.sbuf(f"{pre}_xnc", [128, 8 * QC], BF16)
    wq = p.sbuf(f"{pre}_wq", [128, 8 * 128], BF16)
    wv = p.sbuf(f"{pre}_wv", [128, 8 * NH * 64], BF16)
    qf = [p.sbuf(f"{pre}_qf{j}", [128, S], BF16) for j in range(NH // 2)]
    kf = [p.sbuf(f"{pre}_kf{j}", [128, S], BF16) for j in range(NH // 2)]
    vtm = p.sbuf(f"{pre}_vtm", [128, (S // 128) * NH * 64], BF16)
    vtm3 = vtm[:, :].rearrange("p (tt c) -> p tt c", c=NH * 64)
    vtmp = p.sbuf(f"{pre}_vtmp", [128, NH * 64], F32)
    vbbs = p.sbuf(f"{pre}_vbbs", [128, NH * 64], F32)
    psq = p.psum(f"{pre}_psq", [128, QC], F32)
    psv = p.psum(f"{pre}_psv", [128, NH * 64], F32)
    p.mm(psv[:, :], ones_f1[:, :], vbr[:, :], tag="vbbc")
    p.copy(vbbs[:, :], psv[:, :])
    load_wrows(p, W["qkvT"], 1024, NH * 64, wv[:, :], q="sync",
               col0=2 * NH * 64)
    for c in range(S // QC):
        cs = slice(c * QC, (c + 1) * QC)
        load_wrows(p, xnd, 1024, QC, xnc[:, :], q="sync", col0=c * QC)
        for tt in range(QC // 128):
            for k in range(8):
                p.mm(psv[:, :], xnc[:, k * QC + tt * 128:k * QC + (tt + 1) * 128],
                     wv[:, k * NH * 64:(k + 1) * NH * 64],
                     start=(k == 0), stop=(k == 7))
            p.tt(vtmp[:, :], psv[:, :], vbbs[:, :], ALU.add)
            p.copy(vtm3[:, c * (QC // 128) + tt, :], vtmp[:, :])
        for j in range(NH // 2):
            load_wrows(p, W["qkvT"], 1024, 128, wq[:, :], q="scalar",
                       col0=j * 128)
            for k in range(8):
                p.mm(psq[:, :], wq[:, k * 128:(k + 1) * 128],
                     xnc[:, k * QC:(k + 1) * QC], start=(k == 0), stop=(k == 7))
            p.ts(qf[j][:, cs], psq[:, :], qb[:, j:j + 1], 0.125,
                 op0=ALU.add, op1=ALU.mult)
            load_wrows(p, W["qkvT"], 1024, 128, wq[:, :], q="scalar",
                       col0=NH * 64 + j * 128)
            for k in range(8):
                p.mm(psq[:, :], wq[:, k * 128:(k + 1) * 128],
                     xnc[:, k * QC:(k + 1) * QC], start=(k == 0), stop=(k == 7))
            p.ts(kf[j][:, cs], psq[:, :], kb[:, j:j + 1], op0=ALU.add)
    KG = 4
    ptg = p.sbuf(f"{pre}_ptg", [128, KG * S], BF16)
    zc = p.sbuf(f"{pre}_zc", [128, 8], F32)
    zs = p.sbuf(f"{pre}_zs", [128, 1], F32)
    ctxa = p.sbuf(f"{pre}_ctxa", [64, S], F32)
    ctxb = p.sbuf(f"{pre}_ctxb", [64, 512], BF16)
    pse = p.psum(f"{pre}_pse", [128, QC], F32)
    psx = p.psum(f"{pre}_psx", [64, QC], F32)
    for h in range(NH):
        j, half = h // 2, h % 2
        hrow = slice(64 * half, 64 * half + 64)
        hv = slice(h * 64, (h + 1) * 64)
        first = True
        for kg in range(S // (128 * KG)):
            for kt in range(KG):
                ktile = kg * KG + kt
                taus = slice(ktile * 128, (ktile + 1) * 128)
                for qc2 in range(S // QC):
                    qsl = slice(qc2 * QC, (qc2 + 1) * QC)
                    p.mm(pse[:, :], kf[j][hrow, taus], qf[j][hrow, qsl],
                         tag="sc")
                    p.act(ptg[:, kt * S + qc2 * QC:kt * S + (qc2 + 1) * QC],
                          pse[:, :], AF.Exp, accum_out=zc[:, qc2:qc2 + 1])
                p.red(zs[:, :], zc[:, :], ALU.add)
                p.recip(zs[:, :], zs[:, :])
                p.ts(ptg[:, kt * S:(kt + 1) * S], ptg[:, kt * S:(kt + 1) * S],
                     zs[:, 0:1], op0=ALU.mult)
            for qc2 in range(S // QC):
                qsl = slice(qc2 * QC, (qc2 + 1) * QC)
                for kt in range(KG):
                    ktile = kg * KG + kt
                    p.mm(psx[:, :], vtm3[:, ktile, hv],
                         ptg[:, kt * S + qc2 * QC:kt * S + (qc2 + 1) * QC],
                         start=(kt == 0), stop=(kt == KG - 1))
                if first:
                    p.copy(ctxa[:, qsl], psx[:, :])
                else:
                    p.tt(ctxa[:, qsl], ctxa[:, qsl], psx[:, :], ALU.add)
            first = False
        for qc2 in range(S // 512):
            qsl = slice(qc2 * 512, (qc2 + 1) * 512)
            p.copy(ctxb[:, :], ctxa[:, qsl])
            p.dma("gpsimd", ctx_out.ap()[h * 64:(h + 1) * 64, qsl], ctxb[:, :])
    p.free(qb, kb, vbr, xnc, wq, wv, *qf, *kf, vtm, vtmp, vbbs, psq, psv,
           ptg, zc, zs, ctxa, ctxb, pse, psx)


def emit_l2_attn_a_ctx(p, pre, x_fm, ctx_out, W, S=4096, NZ=32):
    CC = 256
    NBLK = (NZ + 2) // 3
    ones_f1 = p.const_ones(F32, 1, 128)
    xnd = p.dram(f"{pre}_xnd", [1024, S], BF16)
    ln_fm(p, f"{pre}_ln", x_fm, S, xn_bf=None, xn_dram=xnd)
    kvtd = p.dram(f"{pre}_kvtd", [128, (S // 128) * NZ * 33], BF16)
    qfm = [p.sbuf(f"{pre}_qfm{bk}", [128, S], BF16) for bk in range(NBLK)]
    kfmd = p.dram(f"{pre}_kfmd", [NBLK * 128, S], BF16)
    kfm_cur = p.sbuf(f"{pre}_kfmc", [128, S], BF16)
    kstg = p.sbuf(f"{pre}_kstg", [128, 128], BF16)
    becol = p.sbuf(f"{pre}_becol", [128, 1], F32)
    bbs = p.sbuf(f"{pre}_bbs", [128, NZ * 48], F32)
    bapb = W["qkvb_row"].ap()
    p.dma("sync", bbs[:, :],
          bass.AP(bapb.tensor, bapb.offset, [[0, 128], [1, NZ * 48]]))
    bapbe = W["beta"].ap()
    p.dma("sync", becol[:, :],
          bass.AP(bapbe.tensor, bapbe.offset, [[0, 128], [1, 1]]))
    p.act(becol[:, :], becol[:, :], AF.Exp)
    xnc = p.sbuf(f"{pre}_xnc", [128, 8 * 128], BF16)
    wqkv = p.sbuf(f"{pre}_wqkv", [128, 8 * 512], BF16)
    kvt_tt = p.sbuf(f"{pre}_kvtt", [128, NZ * 33], BF16)
    kvz3 = kvt_tt[:, :].rearrange("p (z c) -> p z c", c=33)
    qpad = p.sbuf(f"{pre}_qpad", [128, NBLK * 128], BF16)
    kpad = p.sbuf(f"{pre}_kpad", [128, NBLK * 128], BF16)
    tmf = p.sbuf(f"{pre}_tmf", [128, 512], F32)
    mx = p.sbuf(f"{pre}_mx", [128, NZ], F32)
    sm = p.sbuf(f"{pre}_sm", [128, NZ], F32)
    pstm = p.psum(f"{pre}_pstm", [128, 512], F32)
    pstr = p.psum(f"{pre}_pstr", [128, 128], BF16)
    p.memset(qpad[:, :], 0.0)
    p.memset(kpad[:, :], 0.0)
    p.memset(kvz3[:, :, 32:33], 1.0)
    wsec = []
    for tt in range(S // 128):
        ts128 = slice(tt * 128, (tt + 1) * 128)
        load_wrows(p, xnd, 1024, 128, xnc[:, :], q="sync", col0=tt * 128)
        for sec in range(3):
            load_wrows(p, W["qkvT"], 1024, 512, wqkv[:, :], q="scalar",
                       col0=sec * 512)
            for k in range(8):
                p.mm(pstm[:, :], xnc[:, k * 128:(k + 1) * 128],
                     wqkv[:, k * 512:(k + 1) * 512],
                     start=(k == 0), stop=(k == 7))
            p.tt(tmf[:, :], pstm[:, :], bbs[:, sec * 512:(sec + 1) * 512],
                 ALU.add)
            t3 = tmf[:, :].rearrange("p (z c) -> p z c", c=16)
            if sec == 2:
                p.copy(kvz3[:, :, 16:32], t3)
                p.dma("gpsimd", kvtd.ap()[:, tt * NZ * 33:(tt + 1) * NZ * 33],
                      kvt_tt[:, :])
                continue
            p.ts(tmf[:, :], tmf[:, :], becol[:, 0:1], op0=ALU.mult)
            p.red(mx[:, :], t3, ALU.max, negate=True)
            p.tt(t3, t3, mx[:, :].unsqueeze(2).to_broadcast([128, NZ, 16]),
                 ALU.add)
            p.act(tmf[:, :], tmf[:, :], AF.Exp)
            p.red(sm[:, :], t3, ALU.add)
            p.recip(sm[:, :], sm[:, :])
            p.tt(t3, t3, sm[:, :].unsqueeze(2).to_broadcast([128, NZ, 16]),
                 ALU.mult)
            dstpad = qpad if sec == 0 else kpad
            for bk in range(NBLK):
                nhb = min(3, NZ - bk * 3)
                src = t3[:, bk * 3:bk * 3 + nhb, :]
                dst = dstpad[:, :].rearrange("p (b s c) -> p b s c", s=4,
                                             c=32)[:, bk, 0:nhb, 0:16]
                p.copy(dst, src)
            if sec == 1:
                p.copy(kvz3[:, :, 0:16], t3)
        for bk in range(NBLK):
            p.transpose(pstr[:, :], qpad[:, bk * 128:(bk + 1) * 128])
            p.copy(qfm[bk][:, ts128], pstr[:, :])
            p.transpose(pstr[:, :], kpad[:, bk * 128:(bk + 1) * 128])
            p.copy(kstg[:, :], pstr[:, :])
            p.dma("gpsimd", kfmd.ap()[bk * 128:(bk + 1) * 128, ts128],
                  kstg[:, :])
    p.free(bbs, xnc, wqkv, kvt_tt, qpad, kpad, tmf, mx, sm, pstm, pstr)
    amask = p.sbuf(f"{pre}_amask", [128, 2 * CC], BF16)
    am2 = amask[:, :].rearrange("p (ti c) -> p ti c", c=CC)
    rowi = p.sbuf(f"{pre}_rowi", [128, CC], F32)
    coli = p.sbuf(f"{pre}_coli", [128, CC], F32)
    p.iota(coli[:, :], [[1, CC]], base=0, channel_multiplier=0)
    for ti in range(2):
        p.iota(rowi[:, :], [[0, CC]], base=ti * 128, channel_multiplier=1)
        p.tt(am2[:, ti, :], rowi[:, :], coli[:, :], ALU.is_le)
    kvz = p.sbuf(f"{pre}_kvz", [128, (S // 128) * 33], BF16)
    kvz4 = kvz[:, :].rearrange("p (tt c) -> p tt c", c=33)
    asb = p.sbuf(f"{pre}_asb", [128, 2 * CC], BF16)
    Ssb = p.sbuf(f"{pre}_Ssb", [128, 17], BF16)
    nd = p.sbuf(f"{pre}_nd", [16, S], BF16)
    den = p.sbuf(f"{pre}_den", [1, S], F32)
    ctxt = p.sbuf(f"{pre}_ctxt", [16, 512], BF16)
    psa = p.psum(f"{pre}_psa", [128, CC], F32)
    pso = p.psum(f"{pre}_pso", [128, CC], F32)
    psd1 = p.psum(f"{pre}_psd1", [1, CC], F32)
    psS = p.psum(f"{pre}_psS", [128, 17], F32)
    dbc = p.psum(f"{pre}_dbc", [16, 512], F32)
    nch = S // CC
    for z in range(NZ):
        bk, slot = z // 3, z % 3
        sb = 32 * slot
        srow = slice(sb, sb + 16)
        if slot == 0:
            p.dma("scalar", kfm_cur[:, :], kfmd.ap()[bk * 128:(bk + 1) * 128, :])
        src3 = kvtd.ap().rearrange("p (tt z c) -> p tt z c", z=NZ, c=33)[
            :, :, z, :]
        p.dma("sync", kvz4[:, :, :], src3)
        for c in range(nch):
            cs = slice(c * CC, (c + 1) * CC)
            for ti in range(2):
                taus = slice(c * CC + ti * 128, c * CC + (ti + 1) * 128)
                p.mm(psa[:, :], kfm_cur[srow, taus], qfm[bk][srow, cs],
                     tag="a2t")
                p.tt(asb[:, ti * CC:(ti + 1) * CC], psa[:, :],
                     am2[:, ti, :], ALU.mult)
            for ti in range(2):
                tile_i = c * 2 + ti
                p.mm(pso[0:16, :], kvz4[:, tile_i, 16:32],
                     asb[:, ti * CC:(ti + 1) * CC],
                     start=(ti == 0), stop=(c == 0 and ti == 1))
                p.mm(psd1[0:1, :], kvz4[:, tile_i, 32:33],
                     asb[:, ti * CC:(ti + 1) * CC],
                     start=(ti == 0), stop=(c == 0 and ti == 1))
            if c > 0:
                p.mm(pso[0:16, :], Ssb[srow, 0:16], qfm[bk][srow, cs],
                     start=False, stop=True)
                p.mm(psd1[0:1, :], Ssb[srow, 16:17], qfm[bk][srow, cs],
                     start=False, stop=True)
            for ti in range(2):
                tile_i = c * 2 + ti
                p.mm(psS[srow, :], kvz4[:, tile_i, 0:16],
                     kvz4[:, tile_i, 16:33],
                     start=(c == 0 and ti == 0),
                     stop=(c == nch - 1 and ti == 1))
            if c < nch - 1:
                p.copy(Ssb[srow, :], psS[srow, :])
            p.copy(nd[:, cs], pso[0:16, :])
            p.copy(den[:, cs], psd1[0:1, :], engine="scalar")
        p.recip(den[:, :], den[:, :])
        for c8 in range(8):
            c8s = slice(c8 * 512, (c8 + 1) * 512)
            p.mm(dbc[:, :], ones_f1[:, 0:16], den[:, c8s], tag="dnbc")
            p.tt(ctxt[:, :], nd[:, c8s], dbc[:, :], ALU.mult)
            p.dma("gpsimd", ctx_out.ap()[z * 16:(z + 1) * 16, c8s], ctxt[:, :])
    p.free(amask, rowi, coli, kvz, asb, Ssb, nd, den, ctxt, psa, pso,
           psd1, psS, dbc)
    p.free(*qfm, kfm_cur, kstg, becol)


def emit_l0_attn_b_ctx(p, pre, x_fm, ctx_out, W, S=4096, NH=8, KEEP=391,
                       REC=64):
    KC = 512
    NT = S // 128
    ones_f1 = p.const_ones(F32, 1, 128)
    ones_f128 = p.const_ones(F32, 128, 1)
    xnd = p.dram(f"{pre}_xnd", [1024, S], BF16)
    ln_fm(p, f"{pre}_ln", x_fm, S, xn_bf=None, xn_dram=xnd)
    kvd = p.dram(f"{pre}_kvd", [S, NH * 128], BF16)
    impd = p.dram(f"{pre}_impd", [1, S], F32)
    posd = p.dram(f"{pre}_posd", [1, S], F32)
    qf = [p.sbuf(f"{pre}_qf{j}", [128, S], BF16) for j in range(NH // 2)]
    kf = [p.sbuf(f"{pre}_kf{j}", [128, S], BF16) for j in range(NH // 2)]
    klo = [p.sbuf(f"{pre}_klo{j}", [128, S], BF16) for j in range(NH // 2)]
    kfull = p.sbuf(f"{pre}_kfull", [128, KC], F32)
    kback = p.sbuf(f"{pre}_kback", [128, KC], F32)
    xnc = p.sbuf(f"{pre}_xnc", [128, 8 * KC], BF16)
    wq = p.sbuf(f"{pre}_wq", [128, 8 * 128], BF16)
    wkv = p.sbuf(f"{pre}_wkv", [128, 8 * NH * 64], BF16)
    wvv = p.sbuf(f"{pre}_wvv", [128, 8 * NH * 64], BF16)
    kvs = p.sbuf(f"{pre}_kvs", [128, NH * 64], BF16)
    psq = p.psum(f"{pre}_psq", [128, KC], F32)
    load_wrows(p, W["qkvT"], 1024, NH * 64, wkv[:, :], q="sync", col0=NH * 64)
    load_wrows(p, W["qkvT"], 1024, NH * 64, wvv[:, :], q="sync",
               col0=2 * NH * 64)
    scale = 1.0 / 8.0
    for c in range(S // KC):
        cs = slice(c * KC, (c + 1) * KC)
        load_wrows(p, xnd, 1024, KC, xnc[:, :], q="sync", col0=c * KC)
        for j in range(NH // 2):
            load_wrows(p, W["qkvT"], 1024, 128, wq[:, :], q="scalar",
                       col0=j * 128)
            for k in range(8):
                p.mm(psq[:, :], wq[:, k * 128:(k + 1) * 128],
                     xnc[:, k * KC:(k + 1) * KC], start=(k == 0), stop=(k == 7))
            p.ts(qf[j][:, cs], psq[:, :], scale, op0=ALU.mult)
            load_wrows(p, W["qkvT"], 1024, 128, wq[:, :], q="scalar",
                       col0=NH * 64 + j * 128)
            for k in range(8):
                p.mm(psq[:, :], wq[:, k * 128:(k + 1) * 128],
                     xnc[:, k * KC:(k + 1) * KC], start=(k == 0), stop=(k == 7))
            p.copy(kfull[:, :], psq[:, :])
            p.copy(kf[j][:, cs], kfull[:, :])
            p.copy(kback[:, :], kf[j][:, cs])
            p.tt(kback[:, :], kfull[:, :], kback[:, :], ALU.subtract)
            p.copy(klo[j][:, cs], kback[:, :])
        for tt in range(KC // 128):
            gtt = c * (KC // 128) + tt
            for sec, wsb in ((0, wkv), (1, wvv)):
                for k in range(8):
                    p.mm(psq[:, 0:NH * 64],
                         xnc[:, k * KC + tt * 128:k * KC + (tt + 1) * 128],
                         wsb[:, k * NH * 64:(k + 1) * NH * 64],
                         start=(k == 0), stop=(k == 7))
                p.copy(kvs[:, :], psq[:, 0:NH * 64])
                dst = kvd.ap()[gtt * 128:(gtt + 1) * 128, :].rearrange(
                    "t (h c) -> t h c", c=128)[:, :, sec * 64:(sec + 1) * 64]
                p.dma("gpsimd", dst,
                      kvs[:, :].rearrange("p (h c) -> p h c", c=64))
    p.free(xnc, wq, wkv, wvv, kvs, psq, kfull, kback)
    dmask = p.sbuf(f"{pre}_dmask", [128, 4 * KC], BF16)
    dm4 = dmask[:, :].rearrange("p (f c) -> p f c", c=KC)
    rowi = p.sbuf(f"{pre}_rowi", [128, KC], F32)
    coli = p.sbuf(f"{pre}_coli", [128, KC], F32)
    p.iota(coli[:, :], [[1, KC]], base=0, channel_multiplier=0)
    for ph in range(4):
        p.iota(rowi[:, :], [[0, KC]], base=ph * 128, channel_multiplier=1)
        p.tt(dm4[:, ph, :], coli[:, :], rowi[:, :], ALU.is_le)
    jrow1 = p.sbuf(f"{pre}_jrow1", [128, 512], F32)
    p.iota(jrow1[:, :], [[1, 512]], base=1, channel_multiplier=0)
    jvf = p.sbuf(f"{pre}_jvf", [128, 4], F32)
    jval = p.sbuf(f"{pre}_jval", [128, 4], BF16)
    for jt in range(4):
        p.iota(jvf[:, jt:jt + 1], [[0, 1]], base=jt * 128, channel_multiplier=1)
    p.ts(jval[:, :], jvf[:, :], float(KEEP) - 0.5, op0=ALU.is_lt)
    e1t = p.sbuf(f"{pre}_e1t", [128, S], BF16)
    zp = p.sbuf(f"{pre}_zp", [128, 8], F32)
    z1 = p.sbuf(f"{pre}_z1", [128, 1], F32)
    rzw = p.sbuf(f"{pre}_rzw", [128, 1], BF16)
    rzlo = p.sbuf(f"{pre}_rzlo", [128, 1], BF16)
    rzb = p.sbuf(f"{pre}_rzb", [128, 1], F32)
    imp7 = p.sbuf(f"{pre}_imp7", [1, 512], F32)
    imps = p.sbuf(f"{pre}_imps", [1, S], F32)
    posr = p.sbuf(f"{pre}_posr", [1, S], F32)
    imp2d = p.sbuf(f"{pre}_imp2d", [32, 128], F32)
    thr = p.sbuf(f"{pre}_thr", [32, 4], F32)
    cntc = p.sbuf(f"{pre}_cntc", [32, 1], F32)
    maskt = p.sbuf(f"{pre}_maskt", [32, 128], F32)
    posc = p.sbuf(f"{pre}_posc", [128, 32], F32)
    selt = p.sbuf(f"{pre}_selt", [128, 512], BF16)
    kvh = p.sbuf(f"{pre}_kvh", [128, 32 * 128], BF16)
    kvh3 = kvh[:, :].rearrange("p (tt c) -> p tt c", c=128)
    ksel = p.sbuf(f"{pre}_ksel", [128, 4 * 128], BF16)
    ksel3 = ksel[:, :].rearrange("p (jt c) -> p jt c", c=128)
    kself = p.sbuf(f"{pre}_kself", [128, 512], BF16)
    e2t = [p.sbuf(f"{pre}_e2t{jt}", [128, KC], BF16) for jt in range(4)]
    z2 = p.sbuf(f"{pre}_z2", [1, KC], F32)
    zbc = p.sbuf(f"{pre}_zbc", [64, KC], F32)
    ctx2 = p.sbuf(f"{pre}_ctx2", [64, KC], BF16)
    cnts = p.sbuf(f"{pre}_cnts", [1, 1], F32)
    pss = p.psum(f"{pre}_pss", [128, KC], F32)
    for h in range(NH):
        j2, half = h // 2, h % 2
        hrow = slice(64 * half, 64 * half + 64)
        psi = p.psum(f"{pre}_psi{h}", [1, 3584], F32)
        for qt in range(NT):
            nkc = qt // 4 + 1
            for kc in range(nkc):
                ksl = slice(kc * KC, (kc + 1) * KC)
                p.mm(pss[:, :], qf[j2][hrow, qt * 128:(qt + 1) * 128],
                     kf[j2][hrow, ksl], start=True, stop=False, tag="s1")
                p.mm(pss[:, :], qf[j2][hrow, qt * 128:(qt + 1) * 128],
                     klo[j2][hrow, ksl], start=False, stop=True, tag="s1lo")
                if kc == qt // 4:
                    p.act(e1t[:, ksl], pss[:, :], AF.Exp)
                    p.op("vector",
                         lambda e, _o=e1t[:, ksl], _m=dm4[:, qt % 4, :],
                         _a=zp[:, kc:kc + 1]:
                         e.scalar_tensor_tensor(_o, _o, 1.0, _m, ALU.mult,
                                                ALU.mult, accum_out=_a),
                         reads=[e1t[:, ksl], dm4[:, qt % 4, :]],
                         writes=[e1t[:, ksl], zp[:, kc:kc + 1]], tag="dmsk")
                else:
                    p.act(e1t[:, ksl], pss[:, :], AF.Exp,
                          accum_out=zp[:, kc:kc + 1])
            p.red(z1[:, :], zp[:, 0:nkc], ALU.add)
            p.recip(z1[:, :], z1[:, :])
            p.ts(z1[:, :], z1[:, :], 1.0 / float(S), op0=ALU.mult)
            p.copy(rzw[:, :], z1[:, :])
            p.copy(rzb[:, :], rzw[:, :])
            p.tt(rzb[:, :], z1[:, :], rzb[:, :], ALU.subtract)
            p.copy(rzlo[:, :], rzb[:, :])
            for kc in range(nkc):
                if kc < 7:
                    p.mm(psi[0:1, kc * KC:(kc + 1) * KC], rzw[:, :],
                         e1t[:, kc * KC:(kc + 1) * KC],
                         start=(qt == 4 * kc), stop=False)
                    p.mm(psi[0:1, kc * KC:(kc + 1) * KC], rzlo[:, :],
                         e1t[:, kc * KC:(kc + 1) * KC],
                         start=False, stop=(qt == NT - 1))
                else:
                    p.mm(pss[0:1, 0:KC], rzw[:, :],
                         e1t[:, kc * KC:(kc + 1) * KC], start=True, stop=False,
                         tag="imp7")
                    p.mm(pss[0:1, 0:KC], rzlo[:, :],
                         e1t[:, kc * KC:(kc + 1) * KC], start=False, stop=True,
                         tag="imp7b")
                    if qt == 28:
                        p.copy(imp7[:, :], pss[0:1, 0:KC])
                    else:
                        p.tt(imp7[:, :], imp7[:, :], pss[0:1, 0:KC], ALU.add)
        p.copy(imps[:, 0:3584], psi[0:1, :])
        p.copy(imps[:, 3584:4096], imp7[:, :])
        p.memset(imps[:, S - REC:S], 1e30)
        p.free(psi)
        p.dma("gpsimd", impd.ap(), imps[:, :])
        p.dma("sync", imp2d[:, :],
              bass.AP(impd.ap().tensor, 0, [[128, 32], [1, 128]]))
        pscn = p.psum(f"{pre}_pscn{h}", [32, 1], F32)
        p.memset(thr[:, 0:1], 0.75)
        p.memset(thr[:, 1:2], 0.375)
        for it in range(25):
            p.ts(maskt[:, :], imp2d[:, :], thr[:, 0:1], op0=ALU.is_ge,
                 tag="bis")
            p.red(cntc[:, :], maskt[:, :], ALU.add)
            p.mm(pscn[0:1, :], cntc[:, :], ones_f128[0:32, 0:1], tag="cnt")
            p.mm(pscn[0:32, 0:1], ones_f1[:, 0:32], pscn[0:1, 0:1],
                 tag="cntbc", skip_group=True) if False else None
            p.copy(cnts[:, :], pscn[0:1, 0:1])
            p.mm(pscn[:, :], ones_f1[:, 0:32], cnts[:, :], tag="cntbc")
            p.copy(thr[:, 2:3], pscn[:, 0:1])
            p.ts(thr[:, 3:4], thr[:, 2:3], float(KEEP) + 0.5, op0=ALU.is_gt)
            p.ts(thr[:, 3:4], thr[:, 3:4], 2.0, -1.0, op0=ALU.mult,
                 op1=ALU.add)
            p.tt(thr[:, 3:4], thr[:, 3:4], thr[:, 1:2], ALU.mult)
            p.tt(thr[:, 0:1], thr[:, 0:1], thr[:, 3:4], ALU.add)
            p.ts(thr[:, 1:2], thr[:, 1:2], 0.5, op0=ALU.mult)
        p.free(pscn)
        p.ts(imps[:, :], imps[:, :], thr[0:1, 0:1], op0=ALU.is_ge)
        p.scan(posr[:, :], imps[:, :], imps[:, :], 0.0, ALU.add, ALU.bypass)
        p.tt(posr[:, :], posr[:, :], imps[:, :], ALU.mult)
        p.dma("gpsimd", posd.ap(), posr[:, :])
        p.dma("sync", imp2d[:, :],
              bass.AP(posd.ap().tensor, 0, [[128, 32], [1, 128]]))
        pstp = p.psum(f"{pre}_pstp{h}", [128, 32], F32)
        p.transpose(pstp[:, :], imp2d[:, :])
        p.copy(posc[:, :], pstp[:, :])
        p.free(pstp)
        p.dma("sync", kvh[:, :],
              kvd.ap().rearrange("(tt pp) hc -> pp tt hc", pp=128)[
                  :, :, h * 128:(h + 1) * 128])
        pk4 = [p.psum(f"{pre}_pk4_{h}_{jt}", [128, 128], F32)
               for jt in range(4)]
        for kt in range(32):
            p.ts(selt[:, :], jrow1[:, :], posc[:, kt:kt + 1],
                 op0=ALU.is_equal, tag="selt")
            for jt in range(4):
                p.mm(pk4[jt][:, :], selt[:, jt * 128:(jt + 1) * 128],
                     kvh3[:, kt, :], start=(kt == 0), stop=(kt == 31))
        pstr = p.psum(f"{pre}_pstr{h}", [128, 128], BF16)
        for jt in range(4):
            p.copy(ksel3[:, jt, :], pk4[jt][:, :])
            p.transpose(pstr[hrow, :], ksel3[:, jt, 0:64])
            p.copy(kself[hrow, jt * 128:(jt + 1) * 128], pstr[hrow, :])
        p.free(pstr, *pk4)
        pse = p.psum(f"{pre}_pse{h}", [128, KC], F32)
        psz = p.psum(f"{pre}_psz{h}", [1, KC], F32)
        psx = p.psum(f"{pre}_psx{h}", [64, KC], F32)
        psb = p.psum(f"{pre}_psb{h}", [64, KC], F32)
        for qc in range(S // KC):
            qsl = slice(qc * KC, (qc + 1) * KC)
            for jt in range(4):
                p.mm(pse[:, :], kself[hrow, jt * 128:(jt + 1) * 128],
                     qf[j2][hrow, qsl], tag="s2")
                p.act(e2t[jt][:, :], pse[:, :], AF.Exp)
                p.mm(psz[0:1, :], jval[:, jt:jt + 1], e2t[jt][:, :],
                     start=(jt == 0), stop=(jt == 3))
                p.mm(psx[:, :], ksel3[:, jt, 64:128], e2t[jt][:, :],
                     start=(jt == 0), stop=(jt == 3))
            p.copy(z2[:, :], psz[0:1, :], engine="scalar")
            p.recip(z2[:, :], z2[:, :])
            p.mm(psb[:, :], ones_f1[:, 0:64], z2[:, :], tag="z2bc")
            p.copy(zbc[:, :], psb[:, :])
            p.tt(ctx2[:, :], psx[:, :], zbc[:, :], ALU.mult)
            p.dma("gpsimd", ctx_out.ap()[h * 64:(h + 1) * 64, qsl],
                  ctx2[:, :])
        p.free(psb, psx, psz, pse)
    p.free(dmask, rowi, coli, jrow1, jvf, jval, e1t, zp, z1, rzw, rzlo,
           rzb, imp7, imps,
           posr, imp2d, thr, cntc, maskt, posc, selt, kvh, ksel,
           kself, *e2t, z2, zbc, ctx2, cnts, pss)
    p.free(*klo, *kf, *qf)


def _rms_fm(p, pre, src_rows, T, out_bf, wcol, eps, nchunk=512, out_dram=None,
            rsum_out=None):
    ones_bf = p.const_ones(BF16, 128, 1)
    ones_f1 = p.const_ones(F32, 1, 128)
    xt = [p.sbuf(f"{pre}_xt{i}", [128, nchunk], F32) for i in range(2)]
    xq = p.sbuf(f"{pre}_xq", [128, nchunk], BF16)
    st1 = p.psum(f"{pre}_st1", [1, nchunk], F32)
    bc1 = p.psum(f"{pre}_bc1", [128, nchunk], F32)
    ms = p.sbuf(f"{pre}_ms", [1, nchunk], F32)
    tmp = p.sbuf(f"{pre}_tmp", [128, nchunk], F32)
    for c in range(T // nchunk):
        cs = slice(c * nchunk, (c + 1) * nchunk)
        for k in range(8):
            dst = xt[k % 2][:, :]
            p.dma("sync", dst, src_rows[k][:, cs])
            p.act(xq[:, :], dst, AF.Square)
            p.mm(st1[0:1, :], ones_bf[:, :], xq[:, :], start=(k == 0),
                 stop=(k == 7))
        p.ts(ms[:, :], st1[0:1, :], 1.0 / 1024.0, float(eps), op0=ALU.mult,
             op1=ALU.add)
        p.act(ms[:, :], ms[:, :], AF.Sqrt)
        p.recip(ms[:, :], ms[:, :])
        p.mm(bc1[:, :], ones_f1[:, :], ms[:, :], tag="rmsbc")
        for k in range(8):
            dst = xt[k % 2][:, :]
            p.dma("sync", dst, src_rows[k][:, cs])
            p.tt(tmp[:, :], dst, bc1[:, :], ALU.mult)
            if wcol is not None:
                p.ts(tmp[:, :], tmp[:, :], wcol[:, k:k + 1], op0=ALU.mult)
            if out_bf is not None:
                p.copy(out_bf[k][:, cs], tmp[:, :])
            if out_dram is not None:
                p.dma("gpsimd", out_dram.ap()[k * 128:(k + 1) * 128, cs],
                      tmp[:, :])
            if rsum_out is not None:
                p.red(tmp[:, 0:1], tmp[:, :], ALU.add)
                if c == 0:
                    p.copy(rsum_out[:, k:k + 1], tmp[:, 0:1])
                else:
                    p.tt(rsum_out[:, k:k + 1], rsum_out[:, k:k + 1],
                         tmp[:, 0:1], ALU.add)
    p.free(*xt, xq, st1, bc1, ms, tmp)


def emit_l1b_A(p, pre, x_rows, outs, W, T):
    win = p.sbuf(f"{pre}_win", [128, 8], F32)
    p.dma("sync", win[:, :], W["w_in"].ap())
    rs = p.sbuf(f"{pre}_rs", [128, 8], F32)
    _rms_fm(p, f"{pre}_rms", x_rows, T, None, win, 1.1920929e-07,
            out_dram=outs["xnd"], rsum_out=rs)
    p.dma("gpsimd", outs["rsum"].ap(), rs[:, :])
    wsl = p.sbuf(f"{pre}_wsl", [128, 8 * 160], BF16)
    load_wrows(p, W["slrT"], 1024, 160, wsl[:, :], q="sync")
    xnc = p.sbuf(f"{pre}_xnc", [128, 8 * 512], BF16)
    pj = p.psum(f"{pre}_pj", [128, 512], F32)
    pj2 = p.psum(f"{pre}_pj2", [32, 512], F32)
    pjs = p.sbuf(f"{pre}_pjs", [128, 512], F32)
    pjs2 = p.sbuf(f"{pre}_pjs2", [32, 512], F32)
    for c in range(T // 512):
        cs = slice(c * 512, (c + 1) * 512)
        load_wrows(p, outs["xnd"], 1024, 512, xnc[:, :], q="gpsimd",
                   col0=c * 512)
        for k in range(8):
            p.mm(pj[:, :], wsl[:, k * 160:k * 160 + 128],
                 xnc[:, k * 512:(k + 1) * 512], start=(k == 0), stop=(k == 7))
            p.mm(pj2[:, :], wsl[:, k * 160 + 128:(k + 1) * 160],
                 xnc[:, k * 512:(k + 1) * 512], start=(k == 0), stop=(k == 7))
        p.copy(pjs[:, :], pj[:, :])
        p.copy(pjs2[:, :], pj2[:, :], engine="scalar")
        p.dma("gpsimd", outs["qkd"].ap()[0:128, cs], pjs[:, :])
        p.dma("gpsimd", outs["qkd"].ap()[128:160, cs], pjs2[:, :])
    p.free(win, rs, wsl, xnc, pj, pj2, pjs, pjs2)


def emit_l1b_B1(p, pre, ins, outs, W, SB=4096, NHH=4):
    TQ = SB // 2
    q0 = W["q0"]
    ones_f1 = p.const_ones(F32, 1, 128)
    ones_bf = p.const_ones(BF16, 128, 1)
    lamc = p.sbuf(f"{pre}_lamc", [128, 8], F32)
    lap = W["lam"].ap()
    p.dma("sync", lamc[:, 0:8],
          bass.AP(lap.tensor, lap.offset, [[0, 128], [1, 8]]))
    d0 = p.sbuf(f"{pre}_d0", [128, SB], F32)
    d1 = p.sbuf(f"{pre}_d1", [128, SB], F32)
    hh = p.sbuf(f"{pre}_hh", [128, SB], F32)
    xnt = p.sbuf(f"{pre}_xnt", [128, SB], BF16)
    onesb = p.sbuf(f"{pre}_onesb", [128, 512], F32)
    p.memset(onesb[:, :], 1.0)
    for h in range(NHH):
        gh = W["h0"] + h
        p.dma("gpsimd", xnt[:, :], ins["xnh"].ap()[h * 128:(h + 1) * 128, :])
        for c in range(SB // 512):
            p.ts(d0[:, c * 512:(c + 1) * 512], onesb[:, :],
                 lamc[:, gh:gh + 1], op0=ALU.mult)
        p.ts(d1[:, :], xnt[:, :], lamc[:, gh:gh + 1], op0=ALU.mult)
        p.tt(d1[:, :], xnt[:, :], d1[:, :], ALU.subtract)
        p.scan(hh[:, :], d0[:, :], d1[:, :], 0.0, ALU.mult, ALU.add)
        p.dma("gpsimd", outs["ema"].ap()[h * 128:(h + 1) * 128, :], hh[:, :])
    p.free(d0, d1, hh, xnt, onesb)
    qs = p.sbuf(f"{pre}_qs", [64, SB], BF16)
    ks = p.sbuf(f"{pre}_ks", [64, SB], BF16)
    vtm = p.sbuf(f"{pre}_vtm", [128, (SB // 128) * 32], BF16)
    vtm3 = vtm[:, :].rearrange("p (tt c) -> p tt c", c=32)
    qkt = p.sbuf(f"{pre}_qkt", [128, 512], F32)
    qkb = p.sbuf(f"{pre}_qkb", [128, 512], BF16)
    qkt2 = p.sbuf(f"{pre}_qkt2", [32, 512], F32)
    qkb2 = p.sbuf(f"{pre}_qkb2", [32, 512], BF16)
    pstr = p.psum(f"{pre}_pstr", [128, 128], BF16)
    for c in range(SB // 512):
        cs = slice(c * 512, (c + 1) * 512)
        p.dma("sync", qkt[:, :], ins["qk"].ap()[0:128, cs])
        p.copy(qkb[:, :], qkt[:, :])
        p.dma("sync", qkt2[:, :], ins["qk"].ap()[128:160, cs])
        p.copy(qkb2[:, :], qkt2[:, :], engine="scalar")
        p.copy(qs[0:32, cs], qkb[0:32, :])
        p.copy(qs[32:64, cs], qkb[32:64, :])
        p.copy(ks[0:32, cs], qkb[64:96, :])
        p.copy(ks[32:64, cs], qkb[96:128, :])
        for tt in range(4):
            p.transpose(pstr[:, 0:32], qkb2[:, tt * 128:(tt + 1) * 128])
            p.copy(vtm3[:, c * 4 + tt, :], pstr[:, 0:32], tag="vcp")
    slam = p.sbuf(f"{pre}_slam", [128, 1], F32)
    slp = W["slrlam"].ap()
    p.dma("sync", slam[:, :],
          bass.AP(slp.tensor, slp.offset, [[0, 128], [1, 1]]))
    p.ts(slam[:, :], slam[:, :], -1.0, op0=ALU.mult)
    e1 = p.sbuf(f"{pre}_e1", [128, 512], BF16)
    zr = p.sbuf(f"{pre}_zr", [1, 512], F32)
    zbc = p.sbuf(f"{pre}_zbc", [32, 512], F32)
    ctx1 = p.sbuf(f"{pre}_ctx1", [32, TQ], F32)
    ctx2 = p.sbuf(f"{pre}_ctx2", [32, TQ], F32)
    cmb = p.sbuf(f"{pre}_cmb", [32, 512], F32)
    pse = p.psum(f"{pre}_pse", [128, 512], F32)
    psz = p.psum(f"{pre}_psz", [1, 512], F32)
    psc = p.psum(f"{pre}_psc", [32, 512], F32)
    psb = p.psum(f"{pre}_psb", [32, 512], F32)
    sc = float(1.0 / np.sqrt(32.0))
    for which, ctx in ((0, ctx1), (1, ctx2)):
        rsl = slice(32 * which, 32 * which + 32)
        for qc in range(TQ // 512):
            qsl = slice(q0 + qc * 512, q0 + (qc + 1) * 512)
            lsl = slice(qc * 512, (qc + 1) * 512)
            for kt in range(SB // 128):
                tks = slice(kt * 128, (kt + 1) * 128)
                p.mm(pse[:, :], ks[rsl, tks], qs[rsl, qsl], tag="se")
                p.act(e1[:, :], pse[:, :], AF.Exp, scale=sc)
                p.mm(psz[0:1, :], ones_bf[:, :], e1[:, :],
                     start=(kt == 0), stop=(kt == SB // 128 - 1))
                p.mm(psc[:, :], vtm3[:, kt, :], e1[:, :],
                     start=(kt == 0), stop=(kt == SB // 128 - 1))
            p.copy(zr[:, :], psz[0:1, :], engine="scalar")
            p.recip(zr[:, :], zr[:, :])
            p.mm(psb[:, :], ones_f1[:, 0:32], zr[:, :], tag="zbc")
            p.copy(zbc[:, :], psb[:, :])
            p.tt(zbc[:, :], zbc[:, :], psc[:, :], ALU.mult)
            p.copy(ctx[:, lsl], zbc[:, :])
    for qc in range(TQ // 512):
        lsl = slice(qc * 512, (qc + 1) * 512)
        p.stt(cmb[:, :], ctx2[:, lsl], slam[0:32, 0:1], ctx1[:, lsl],
              ALU.mult, ALU.add)
        p.dma("gpsimd", outs["slrctx"].ap()[:, lsl], cmb[:, :])
    p.free(qs, ks, vtm, qkt, qkb, qkt2, qkb2, pstr, slam, e1, zr, zbc, ctx1,
           ctx2, cmb, pse, psz, psc, psb, lamc)


def emit_l1b_B2(p, pre, x_rows, ins, outs, W, T):
    NC = 512
    ones_f1 = p.const_ones(F32, 1, 128)
    pcol = p.sbuf(f"{pre}_pcol", [128, 2], F32)
    rin = p.sbuf(f"{pre}_rin", [128, 16], F32)
    p.dma("sync", rin[:, :], ins["rsum"].ap())
    p.tt(rin[:, 0:8], rin[:, 0:8], rin[:, 8:16], ALU.add)
    p.ts(rin[:, 0:8], rin[:, 0:8], 1.0 / 4096.0, op0=ALU.mult)
    rinb = p.sbuf(f"{pre}_rinb", [128, 8], BF16)
    p.copy(rinb[:, :], rin[:, 0:8])
    wr1 = p.sbuf(f"{pre}_wr1", [128, 8 * 512], BF16)
    load_wrows(p, W["rw1T"], 1024, 512, wr1[:, :], q="sync")
    ps1 = p.psum(f"{pre}_ps1", [128, 4], F32)
    h1 = p.sbuf(f"{pre}_h1", [128, 4], F32)
    h1b = p.sbuf(f"{pre}_h1b", [128, 4], BF16)
    for mt in range(4):
        for k in range(8):
            p.mm(ps1[:, mt:mt + 1],
                 wr1[:, k * 512 + mt * 128:k * 512 + (mt + 1) * 128],
                 rinb[:, k:k + 1], start=(k == 0), stop=(k == 7))
    p.act(h1[:, :], ps1[:, :], AF.Silu)
    p.copy(h1b[:, :], h1[:, :])
    wr2 = p.sbuf(f"{pre}_wr2", [128, 4 * 3], BF16)
    load_wrows(p, W["rw2"], 512, 3, wr2[:, :], q="sync")
    ps2 = p.psum(f"{pre}_ps2", [3, 1], F32)
    for k in range(4):
        p.mm(ps2[:, :], wr2[:, k * 3:(k + 1) * 3], h1b[:, k:k + 1],
             start=(k == 0), stop=(k == 3))
    lg3 = p.sbuf(f"{pre}_lg3", [3, 1], BF16)
    p.copy(lg3[:, :], ps2[:, :])
    pst3 = p.psum(f"{pre}_pst3", [1, 3], BF16)
    p.transpose(pst3[:, :], lg3[:, :])
    rb2 = p.sbuf(f"{pre}_rb2", [1, 3], F32)
    p.dma("sync", rb2[:, :], W["rb2"].ap())
    lgr = p.sbuf(f"{pre}_lgr", [1, 3], F32)
    p.tt(lgr[:, :], pst3[:, :], rb2[:, :], ALU.add)
    nmx = p.sbuf(f"{pre}_nmx", [1, 1], F32)
    p.red(nmx[:, :], lgr[:, :], ALU.max, negate=True)
    zz1 = p.sbuf(f"{pre}_zz1", [1, 1], F32)
    p.act(lgr[:, :], lgr[:, :], AF.Exp, bias=nmx[:, 0:1], accum_out=zz1[:, 0:1])
    p.recip(zz1[:, :], zz1[:, :])
    p.ts(lgr[:, :], lgr[:, :], zz1[:, 0:1], op0=ALU.mult)
    pmpf = p.sbuf(f"{pre}_pmpf", [1, 1], F32)
    p.red(pmpf[:, :], lgr[:, 1:3], ALU.add)
    flg = p.sbuf(f"{pre}_flg", [1, 1], F32)
    p.dma("sync", flg[:, :], W["flag"].ap())
    fpf = p.sbuf(f"{pre}_fpf", [1, 1], F32)
    p.tt(fpf[:, :], lgr[:, 2:3], flg[:, :], ALU.mult)
    psb1 = p.psum(f"{pre}_psb1", [128, 2], F32)
    p.mm(psb1[:, 0:1], ones_f1[:, :], pmpf[:, :], tag="pmbc")
    p.mm(psb1[:, 1:2], ones_f1[:, :], fpf[:, :], tag="fpbc")
    p.copy(pcol[:, :], psb1[:, :])
    p.free(rin, rinb, wr1, ps1, h1, h1b, wr2, ps2, lg3, pst3, rb2, lgr, nmx,
           zz1, pmpf, flg, fpf, psb1)
    mixed = [p.sbuf(f"{pre}_mx{k}", [128, T], BF16) for k in range(8)]
    xnb = [p.sbuf(f"{pre}_xb{k}", [128, T], BF16) for k in range(8)]
    sct = p.sbuf(f"{pre}_sct", [32, T], BF16)
    sf = p.sbuf(f"{pre}_sf", [32, NC], F32)
    wos = p.sbuf(f"{pre}_wos", [32, 1024], BF16)
    emt = p.sbuf(f"{pre}_emt", [128, NC], F32)
    xf32 = p.sbuf(f"{pre}_xf32", [128, NC], F32)
    psm = p.psum(f"{pre}_psm", [128, NC], F32)
    p.dma("sync", wos[:, :], W["woT"].ap())
    for c in range(T // NC):
        cs = slice(c * NC, (c + 1) * NC)
        p.dma("sync", sf[:, :], ins["slrctx"].ap()[:, cs])
        p.copy(sct[:, cs], sf[:, :])
        for k in range(8):
            p.dma("scalar", xf32[:, :], ins["xnd"].ap()[k * 128:(k + 1) * 128, cs])
            p.copy(xnb[k][:, cs], xf32[:, :])
    for mt in range(8):
        for c in range(T // NC):
            cs = slice(c * NC, (c + 1) * NC)
            p.mm(psm[:, :], wos[0:32, mt * 128:(mt + 1) * 128], sct[0:32, cs],
                 tag="slro")
            p.dma("sync", emt[:, :], ins["ema"].ap()[mt * 128:(mt + 1) * 128, cs])
            p.stt(emt[:, :], psm[:, :], pcol[:, 0:1], emt[:, :], ALU.mult,
                  ALU.add)
            p.copy(mixed[mt][:, cs], emt[:, :])
    pslm = p.psum(f"{pre}_pslm", [1, 512], F32)
    arow = p.sbuf(f"{pre}_arow", [1, 64], F32)
    arb = p.sbuf(f"{pre}_arb", [1, 64], BF16)
    acol = p.sbuf(f"{pre}_acol", [64, 1], BF16)
    pstl = p.psum(f"{pre}_pstl", [64, 1], BF16)
    lms = p.sbuf(f"{pre}_lms", [64, 1024], BF16)
    arcr = p.sbuf(f"{pre}_arcr", [1, 1024], F32)
    arcd = p.dram(f"{pre}_arcd", [1, 1024], F32)
    arcc = p.sbuf(f"{pre}_arcc", [128, 8], F32)
    p.dma("sync", lms[:, :], W["lms"].ap())
    lmw8 = p.sbuf(f"{pre}_lmw8", [128, 8 * 64], BF16)
    load_wrows(p, W["lmT"], 1024, 64, lmw8[:, :], q="scalar")
    for k in range(8):
        p.mm(pslm[0:1, 0:64], xnb[k][:, T - 1:T], lmw8[:, k * 64:(k + 1) * 64],
             start=(k == 0), stop=(k == 7))
    p.copy(arow[:, :], pslm[0:1, 0:64])
    nmx2 = p.sbuf(f"{pre}_nmx2", [1, 1], F32)
    p.red(nmx2[:, :], arow[:, :], ALU.max, negate=True)
    p.ts(nmx2[:, :], nmx2[:, :], 1.0 / 32.0, op0=ALU.mult)
    zz2 = p.sbuf(f"{pre}_zz2", [1, 1], F32)
    p.act(arow[:, :], arow[:, :], AF.Exp, bias=nmx2[:, 0:1], scale=1.0 / 32.0,
          accum_out=zz2[:, 0:1])
    p.recip(zz2[:, :], zz2[:, :])
    p.ts(arow[:, :], arow[:, :], zz2[:, 0:1], op0=ALU.mult)
    p.copy(arb[:, :], arow[:, :])
    p.transpose(pstl[:, :], arb[:, :])
    p.copy(acol[:, :], pstl[:, :])
    for half in range(2):
        p.mm(pslm[0:1, :], acol[:, :], lms[:, half * 512:(half + 1) * 512],
             tag="arc")
        p.copy(arcr[:, half * 512:(half + 1) * 512], pslm[0:1, :])
    p.dma("gpsimd", arcd.ap(), arcr[:, :])
    arc8 = p.sbuf(f"{pre}_arc8", [8, 128], F32)
    p.dma("sync", arc8[:, :],
          bass.AP(arcd.ap().tensor, 0, [[128, 8], [1, 128]]))
    pstc = p.psum(f"{pre}_pstc", [128, 8], F32)
    p.transpose(pstc[:, :], arc8[:, :])
    p.copy(arcc[:, :], pstc[:, :])
    p.free(pstc, arc8)
    mcol = p.sbuf(f"{pre}_mcol", [128, 1], F32)
    for mt in range(8):
        p.copy(mcol[:, :], mixed[mt][:, T - 1:T])
        p.stt(mcol[:, :], arcc[:, mt:mt + 1], pcol[:, 1:2], mcol[:, :],
              ALU.mult, ALU.add)
        p.copy(mixed[mt][:, T - 1:T], mcol[:, :])
    p.free(pslm, arow, arb, acol, pstl, lms, arcr, arcc, mcol, lmw8,
           nmx2, zz2, sf, wos, emt, xf32, psm, sct)
    mixd = p.dram(f"{pre}_mixd", [1024, T], F32)
    mw = p.sbuf(f"{pre}_mw", [128, 16 * 128], BF16)
    psx = p.psum(f"{pre}_psx", [128, NC], F32)
    mot = p.sbuf(f"{pre}_mot", [128, NC], F32)
    for mt in range(8):
        load_wblk(p, pre, W["mixwT"], 2048, mt, mw[:, :], q="sync")
        for c in range(T // NC):
            cs = slice(c * NC, (c + 1) * NC)
            for kc in range(16):
                src = mixed[kc][:, cs] if kc < 8 else xnb[kc - 8][:, cs]
                p.mm(psx[:, :], mw[:, kc * 128:(kc + 1) * 128], src,
                     start=(kc == 0), stop=(kc == 15))
            p.copy(mot[:, :], psx[:, :])
            p.dma("gpsimd", mixd.ap()[mt * 128:(mt + 1) * 128, cs], mot[:, :])
    p.free(mw, psx, mot, *xnb, *mixed)
    wout = p.sbuf(f"{pre}_wout", [128, 8], F32)
    p.dma("sync", wout[:, :], W["wout"].ap())
    mp = p.sbuf(f"{pre}_mp", [128, 8], F32)
    ones_bf = p.const_ones(BF16, 128, 1)
    mt1 = p.sbuf(f"{pre}_mt1", [128, NC], F32)
    mq = p.sbuf(f"{pre}_mq", [128, NC], BF16)
    st1 = p.psum(f"{pre}_st1", [1, NC], F32)
    bc1 = p.psum(f"{pre}_bc1", [128, NC], F32)
    msr = p.sbuf(f"{pre}_msr", [1, NC], F32)
    xrt = p.sbuf(f"{pre}_xrt", [128, NC], F32)
    rsc = p.sbuf(f"{pre}_rsc", [128, 1], F32)
    for c in range(T // NC):
        cs = slice(c * NC, (c + 1) * NC)
        for k in range(8):
            p.dma("sync", mt1[:, :], mixd.ap()[k * 128:(k + 1) * 128, cs])
            p.act(mq[:, :], mt1[:, :], AF.Square)
            p.mm(st1[0:1, :], ones_bf[:, :], mq[:, :], start=(k == 0),
                 stop=(k == 7))
        p.ts(msr[:, :], st1[0:1, :], 1.0 / 1024.0, 1.1920929e-07,
             op0=ALU.mult, op1=ALU.add)
        p.act(msr[:, :], msr[:, :], AF.Sqrt)
        p.recip(msr[:, :], msr[:, :])
        p.mm(bc1[:, :], ones_f1[:, :], msr[:, :], tag="robc")
        for k in range(8):
            p.dma("sync", mt1[:, :], mixd.ap()[k * 128:(k + 1) * 128, cs])
            p.dma("scalar", xrt[:, :], x_rows[k][:, cs])
            p.tt(mt1[:, :], mt1[:, :], bc1[:, :], ALU.mult)
            p.ts(mt1[:, :], mt1[:, :], wout[:, k:k + 1], op0=ALU.mult)
            p.tt(mt1[:, :], mt1[:, :], xrt[:, :], ALU.add)
            p.dma("gpsimd", outs["outpre"].ap()[k * 128:(k + 1) * 128, cs],
                  mt1[:, :])
            p.red(rsc[:, :], mt1[:, :], ALU.add)
            if c == 0:
                p.copy(mp[:, k:k + 1], rsc[:, :])
            else:
                p.tt(mp[:, k:k + 1], mp[:, k:k + 1], rsc[:, :], ALU.add)
    p.dma("gpsimd", outs["mpart"].ap(), mp[:, :])
    p.free(wout, mp, mt1, mq, st1, bc1, msr, xrt, rsc, pcol)


def emit_l1b_B3(p, pre, ins, out_rows, W, T):
    ones_f1 = p.const_ones(F32, 1, 128)
    mo = p.sbuf(f"{pre}_mo", [128, 16], F32)
    p.dma("sync", mo[:, :], ins["msum"].ap())
    p.tt(mo[:, 0:8], mo[:, 0:8], mo[:, 8:16], ALU.add)
    p.ts(mo[:, 0:8], mo[:, 0:8], 1.0 / 4096.0, op0=ALU.mult)
    mob = p.sbuf(f"{pre}_mob", [128, 8], BF16)
    p.copy(mob[:, :], mo[:, 0:8])
    wbc = p.sbuf(f"{pre}_wbc", [128, 8 * 128], BF16)
    load_wrows(p, W["bc"], 1024, 128, wbc[:, :], q="sync")
    pss = p.psum(f"{pre}_pss", [128, 1], F32)
    for k in range(8):
        p.mm(pss[:, :], wbc[:, k * 128:(k + 1) * 128], mob[:, k:k + 1],
             start=(k == 0), stop=(k == 7))
    summ = p.sbuf(f"{pre}_summ", [128, 1], BF16)
    p.copy(summ[:, :], pss[:, :])
    wbp = p.sbuf(f"{pre}_wbp", [128, 128], BF16)
    p.dma("sync", wbp[:, :], W["bp"].ap())
    p.mm(pss[:, :], wbp[:, :], summ[:, :], tag="pub")
    pub = p.sbuf(f"{pre}_pub", [128, 1], BF16)
    p.copy(pub[:, :], pss[:, :])
    wbe = p.sbuf(f"{pre}_wbe", [128, 1024], BF16)
    p.dma("sync", wbe[:, :], W["be"].ap())
    exc = p.sbuf(f"{pre}_exc", [128, 8], F32)
    for mt in range(8):
        p.mm(pss[:, :], wbe[:, mt * 128:(mt + 1) * 128], pub[:, :], tag="exp")
        p.copy(exc[:, mt:mt + 1], pss[:, :])
    p.ts(exc[:, :], exc[:, :], 0.1, op0=ALU.mult)
    ot = p.sbuf(f"{pre}_ot", [128, 512], F32)
    for mt in range(8):
        for c in range(T // 512):
            cs = slice(c * 512, (c + 1) * 512)
            p.dma("sync", ot[:, :], ins["outpre"].ap()[mt * 128:(mt + 1) * 128, cs])
            p.ts(ot[:, :], ot[:, :], exc[:, mt:mt + 1], op0=ALU.add)
            p.dma("gpsimd", out_rows[mt][:, cs], ot[:, :])
    p.free(mo, mob, wbc, pss, summ, wbp, pub, wbe, exc, ot)



def run_mamba(I, X, L):
    XH = 512
    T = T_CORE

    def b_m1():
        p = Prog("m1")
        p.prealloc_consts()
        xh = p.dram("xh", [1024, T + XH], F32, kind="ExternalInput")
        W1 = {}
        ipf = p.dram("ipf", [1024, 4096], F32, kind="ExternalInput")
        ipb = p.dram("ipb", [1024, 4096], BF16)
        emit_cast_dram(p, "ci", ipf, ipb, 1024, 4096)
        xpf = p.dram("xpf", [2048, 96], F32, kind="ExternalInput")
        xpb = p.dram("xpb", [2048, 96], BF16)
        emit_cast_dram(p, "cx", xpf, xpb, 2048, 96)
        dtf = p.dram("dtf", [64, 2048], F32, kind="ExternalInput")
        dtb_ = p.dram("dtb_", [64, 2048], BF16)
        emit_cast_dram(p, "cd", dtf, dtb_, 64, 2048)
        W1["inprojT"] = ipb
        W1["xprojT"] = xpb
        W1["dtprojT"] = dtb_
        W1["convw"] = p.dram("convw", [128, 64], F32, kind="ExternalInput")
        W1["convb"] = p.dram("convb", [128, 16], F32, kind="ExternalInput")
        W1["dtb"] = p.dram("dtbias", [128, 16], F32, kind="ExternalInput")
        outs1 = {
            "u_f32": p.dram("u_f32", [2048, T], F32, kind="ExternalOutput"),
            "delta": p.dram("delta", [2048, T], F32, kind="ExternalOutput"),
            "BC": p.dram("BC", [32, T], F32, kind="ExternalOutput"),
            "zs": p.dram("zs", [2048, T], F32, kind="ExternalOutput"),
        }
        emit_mamba_pre(p, "m1", _mk_fm_rows(xh), outs1, W1, T, XH=XH)
        p.finalize_tail()
        return p.build()

    def b_m2():
        p = Prog("m2")
        p.prealloc_consts()
        ins2 = {
            "delta": p.dram("delta", [256, 16384], F32, kind="ExternalInput"),
            "u": p.dram("u", [256, 16384], F32, kind="ExternalInput"),
            "BC": p.dram("BC", [32, 16384], F32, kind="ExternalInput"),
            "alog": p.dram("alog", [128, 32], F32, kind="ExternalInput"),
        }
        yd = p.dram("y", [256, 16384], F32, kind="ExternalOutput")
        emit_mamba_scan(p, "m2", ins2, yd, SB=4096, NB=4, NCT=2, stages=2)
        p.finalize_tail()
        return p.build()

    def b_m3():
        p = Prog("m3")
        p.prealloc_consts()
        ins3 = {
            "y": p.dram("y", [2048, T], F32, kind="ExternalInput"),
            "u_f32": p.dram("u_f32", [2048, T], F32, kind="ExternalInput"),
            "zs": p.dram("zs", [2048, T], F32, kind="ExternalInput"),
        }
        x3 = p.dram("x3", [1024, T], F32, kind="ExternalInput")
        xo = p.dram("xo", [1024, T], F32, kind="ExternalOutput")
        opf = p.dram("opf", [2048, 1024], F32, kind="ExternalInput")
        opb = p.dram("opb", [2048, 1024], BF16)
        emit_cast_dram(p, "co", opf, opb, 2048, 1024)
        W3 = {"Dcol": p.dram("Dcol", [128, 16], F32, kind="ExternalInput"),
              "outprojT": opb}
        emit_mamba_post(p, "m3", ins3, _mk_fm_rows(x3), _mk_fm_rows(xo), W3, T)
        p.finalize_tail()
        return p.build()

    if L == "L2":
        ipf = np.ascontiguousarray(np.asarray(I["L2_attn_b_in_proj_w"]).T)
        conv_w = np.asarray(I["L2_attn_b_conv_w"])
        conv_b = np.asarray(I["L2_attn_b_conv_b"])
        xpf = np.ascontiguousarray(np.asarray(I["L2_attn_b_x_proj_w"]).T)
        dtf = np.ascontiguousarray(np.asarray(I["L2_attn_b_dt_proj_w"]).T)
        dt_proj_b = np.asarray(I["L2_attn_b_dt_proj_b"])
        A_log = np.asarray(I["L2_attn_b_A_log"])
        D_param = np.asarray(I["L2_attn_b_D"])
        opf = np.ascontiguousarray(np.asarray(I["L2_attn_b_out_proj_w"]).T)
    else:
        ipf = np.asarray(I["L3_attn_b_in_proj_w"])
        conv_w = np.asarray(I["L3_attn_b_conv_w"]).reshape(2048, 4)
        conv_b = np.asarray(I["L3_attn_b_conv_b"])
        xpf = np.asarray(I["L3_attn_b_x_proj_w"])
        dtf = np.asarray(I["L3_attn_b_dt_proj_w"])
        dt_proj_b = np.asarray(I["L3_attn_b_dt_proj_b"])
        A_log = np.asarray(I["L3_attn_b_A_log"])
        D_param = np.asarray(I["L3_attn_b_D_param"])
        opf = np.asarray(I["L3_attn_b_out_proj_w"])

    nc1 = _cached("m1", b_m1)
    convw_cols = np.zeros((128, 64), dtype=np.float32)
    for ct in range(16):
        for j in range(4):
            convw_cols[:, ct * 4 + j] = conv_w[ct * 128:(ct + 1) * 128, j]
    ims1 = []
    for c in range(8):
        b, h = c // 2, c % 2
        cols = _tok_cols(c)
        xs = X[:, cols]
        if h == 0:
            halo = np.zeros((1024, XH), dtype=np.float32)
        else:
            halo = X[:, b * 4096 + 2048 - XH: b * 4096 + 2048]
        ims1.append({
            "xh": np.ascontiguousarray(np.concatenate([halo, xs], axis=1)),
            "ipf": ipf, "xpf": xpf, "dtf": dtf,
            "convw": convw_cols, "convb": _colpack(conv_b, 16),
            "dtbias": _colpack(dt_proj_b, 16),
        })
    res1 = _run(nc1, ims1)
    delta_full = _asm_tok([res1.results[c]["delta"] for c in range(8)], 2048,
                          np.float32)
    u_full = _asm_tok([res1.results[c]["u_f32"] for c in range(8)], 2048,
                      np.float32)
    BC_full = _asm_tok([res1.results[c]["BC"] for c in range(8)], 32,
                       np.float32)

    nc2 = _cached("m2", b_m2)
    ims2 = []
    for c in range(8):
        rows = slice(c * 256, (c + 1) * 256)
        alog_cols = np.zeros((128, 32), dtype=np.float32)
        for ct in range(2):
            for n in range(16):
                alog_cols[:, ct * 16 + n] = A_log[c * 256 + ct * 128:
                                                  c * 256 + (ct + 1) * 128, n]
        ims2.append({"delta": np.ascontiguousarray(delta_full[rows]),
                     "u": np.ascontiguousarray(u_full[rows]),
                     "BC": BC_full, "alog": alog_cols})
    res2 = _run(nc2, ims2)
    y_full = np.concatenate([res2.results[c]["y"] for c in range(8)], axis=0)

    nc3 = _cached("m3", b_m3)
    ims3 = []
    for c in range(8):
        cols = _tok_cols(c)
        ims3.append({"y": np.ascontiguousarray(y_full[:, cols]),
                     "u_f32": res1.results[c]["u_f32"],
                     "zs": res1.results[c]["zs"],
                     "x3": np.ascontiguousarray(X[:, cols]),
                     "opf": opf, "Dcol": _colpack(D_param, 16)})
    res3 = _run(nc3, ims3)
    return _asm_tok([res3.results[c]["xo"] for c in range(8)], 1024,
                    np.float32)


def run_l1_attn_b(I, X):
    T = T_CORE
    SB = 4096

    def b_A():
        p = Prog("l1bA")
        p.prealloc_consts()
        xA = p.dram("xA", [1024, T], F32, kind="ExternalInput")
        WA = {"w_in": p.dram("w_in", [128, 8], F32, kind="ExternalInput")}
        slrf = p.dram("slrf", [1024, 160], F32, kind="ExternalInput")
        slrb = p.dram("slrb", [1024, 160], BF16)
        emit_cast_dram(p, "cs", slrf, slrb, 1024, 160)
        WA["slrT"] = slrb
        outsA = {"xnd": p.dram("xnd", [1024, T], F32, kind="ExternalOutput"),
                 "rsum": p.dram("rsum", [128, 8], F32, kind="ExternalOutput"),
                 "qkd": p.dram("qkd", [160, T], F32, kind="ExternalOutput")}
        emit_l1b_A(p, "a", _mk_fm_rows(xA), outsA, WA, T)
        p.finalize_tail()
        return p.build()

    ncA = _cached("l1bA", b_A)
    slr_cols = np.concatenate(
        [np.asarray(I["L1_attn_b_slr_" + n]) for n in
         ("wq1", "wq2", "wk1", "wk2", "wv")], axis=1)
    w_in_c = _colpack(np.asarray(I["L1_attn_b_w_norm_in"]), 8)
    imsA = []
    for c in range(8):
        imsA.append({"xA": np.ascontiguousarray(X[:, _tok_cols(c)]),
                     "w_in": w_in_c,
                     "slrf": np.ascontiguousarray(slr_cols)})
    resA = _run(ncA, imsA)
    xn_full = _asm_tok([resA.results[c]["xnd"] for c in range(8)], 1024,
                       np.float32)
    qk_full = _asm_tok([resA.results[c]["qkd"] for c in range(8)], 160,
                       np.float32)
    rsums = {}
    for c in range(8):
        rsums[(c // 2, c % 2)] = resA.results[c]["rsum"]

    def b_B1():
        p = Prog("l1bB1")
        p.prealloc_consts()
        insB1 = {"xnh": p.dram("xnh", [512, SB], F32, kind="ExternalInput"),
                 "qk": p.dram("qk", [160, SB], F32, kind="ExternalInput")}
        WB1 = {"lam": p.dram("lam", [1, 8], F32, kind="ExternalInput"),
               "slrlam": p.dram("slrlam", [1, 1], F32, kind="ExternalInput"),
               "h0": 0, "q0": 0}
        outsB1 = {"ema": p.dram("ema", [512, SB], F32, kind="ExternalOutput"),
                  "slrctx": p.dram("slrctx", [32, T], F32,
                                   kind="ExternalOutput")}
        emit_l1b_B1(p, "b1", insB1, outsB1, WB1, SB=SB, NHH=4)
        p.finalize_tail()
        return p.build()

    ncB1 = _cached("l1bB1", b_B1)
    lambdas = np.asarray(I["L1_attn_b_lambdas"]).reshape(8)
    slr_lam = np.asarray(I["L1_attn_b_slr_lam"]).reshape(1, 1)
    imsB1 = []
    for c in range(8):
        b, hf = c // 2, c % 2
        bcols = slice(b * SB, (b + 1) * SB)
        xnh = xn_full[hf * 512:(hf + 1) * 512, bcols]
        qkb = qk_full[:, bcols]
        if hf == 1:
            qkb = np.roll(qkb, -T, axis=1)
        lamr = np.roll(lambdas, -4) if hf == 1 else lambdas
        imsB1.append({"xnh": np.ascontiguousarray(xnh),
                      "qk": np.ascontiguousarray(qkb),
                      "lam": lamr.reshape(1, 8).astype(np.float32),
                      "slrlam": slr_lam.astype(np.float32)})
    resB1 = _run(ncB1, imsB1)
    ema_full = np.zeros((1024, 16384), dtype=np.float32)
    slrctx = {}
    for c in range(8):
        b, hf = c // 2, c % 2
        ema_full[hf * 512:(hf + 1) * 512, b * SB:(b + 1) * SB] = \
            resB1.results[c]["ema"]
        slrctx[(b, hf)] = resB1.results[c]["slrctx"]

    def b_B2():
        p = Prog("l1bB2")
        p.prealloc_consts()
        xB2 = p.dram("xB2", [1024, T], F32, kind="ExternalInput")
        insB2 = {"xnd": p.dram("xnd2", [1024, T], F32, kind="ExternalInput"),
                 "ema": p.dram("ema2", [1024, T], F32, kind="ExternalInput"),
                 "slrctx": p.dram("slrctx2", [32, T], F32,
                                  kind="ExternalInput"),
                 "rsum": p.dram("rsum2", [128, 16], F32,
                                kind="ExternalInput")}
        WB2 = {"rb2": p.dram("rb2", [1, 3], F32, kind="ExternalInput"),
               "wout": p.dram("woutd", [128, 8], F32, kind="ExternalInput"),
               "flag": p.dram("flagd", [1, 1], F32, kind="ExternalInput")}
        for nm, arr_shape in (("rw1T", (1024, 512)), ("rw2", (512, 3)),
                              ("woT", (32, 1024)), ("lmT", (1024, 64)),
                              ("lms", (64, 1024)), ("mixwT", (2048, 1024))):
            f = p.dram(nm + "_f", list(arr_shape), F32, kind="ExternalInput")
            bb = p.dram(nm, list(arr_shape), BF16)
            emit_cast_dram(p, "c" + nm, f, bb, arr_shape[0], arr_shape[1])
            WB2[nm] = bb
        outsB2 = {"outpre": p.dram("outpre", [1024, T], F32,
                                   kind="ExternalOutput"),
                  "mpart": p.dram("mpart", [128, 8], F32,
                                  kind="ExternalOutput")}
        emit_l1b_B2(p, "b2", _mk_fm_rows(xB2), insB2, outsB2, WB2, T)
        p.finalize_tail()
        return p.build()

    ncB2 = _cached("l1bB2", b_B2)
    landmarks = np.asarray(I["L1_attn_b_landmarks"])
    imsB2 = []
    for c in range(8):
        b, hf = c // 2, c % 2
        cols = _tok_cols(c)
        imsB2.append({
            "xB2": np.ascontiguousarray(X[:, cols]),
            "xnd2": np.ascontiguousarray(xn_full[:, cols]),
            "ema2": np.ascontiguousarray(ema_full[:, cols]),
            "slrctx2": slrctx[(b, hf)],
            "rsum2": np.concatenate([rsums[(b, 0)], rsums[(b, 1)]], axis=1),
            "rb2": np.asarray(I["L1_attn_b_router_b2"]).reshape(1, 3),
            "woutd": _colpack(np.asarray(I["L1_attn_b_w_norm_out"]), 8),
            "flagd": np.array([[1.0 if hf == 1 else 0.0]], dtype=np.float32),
            "rw1T_f": np.asarray(I["L1_attn_b_router_w1"]),
            "rw2_f": np.asarray(I["L1_attn_b_router_w2"]),
            "woT_f": np.asarray(I["L1_attn_b_slr_wo"]),
            "lmT_f": np.ascontiguousarray(landmarks.T),
            "lms_f": landmarks,
            "mixwT_f": np.asarray(I["L1_attn_b_mix_w"]),
        })
    resB2 = _run(ncB2, imsB2)
    outpre = {}
    mparts = {}
    for c in range(8):
        b, hf = c // 2, c % 2
        outpre[(b, hf)] = resB2.results[c]["outpre"]
        mparts[(b, hf)] = resB2.results[c]["mpart"]

    def b_B3():
        p = Prog("l1bB3")
        p.prealloc_consts()
        insB3 = {"outpre": p.dram("outpre3", [1024, T], F32,
                                  kind="ExternalInput"),
                 "msum": p.dram("msum", [128, 16], F32,
                                kind="ExternalInput")}
        WB3 = {}
        for nm, shp in (("bc", (1024, 128)), ("bp", (128, 128)),
                        ("be", (128, 1024))):
            f = p.dram(nm + "_f", list(shp), F32, kind="ExternalInput")
            bb = p.dram(nm, list(shp), BF16)
            emit_cast_dram(p, "c" + nm, f, bb, shp[0], shp[1])
            WB3[nm] = bb
        xo3 = p.dram("xo3", [1024, T], F32, kind="ExternalOutput")
        emit_l1b_B3(p, "b3", insB3, _mk_fm_rows(xo3), WB3, T)
        p.finalize_tail()
        return p.build()

    ncB3 = _cached("l1bB3", b_B3)
    imsB3 = []
    for c in range(8):
        b, hf = c // 2, c % 2
        imsB3.append({"outpre3": outpre[(b, hf)],
                      "msum": np.concatenate([mparts[(b, 0)], mparts[(b, 1)]],
                                             axis=1),
                      "bc_f": np.asarray(I["L1_attn_b_bus_compress"]),
                      "bp_f": np.asarray(I["L1_attn_b_bus_publish"]),
                      "be_f": np.asarray(I["L1_attn_b_bus_expand"])})
    resB3 = _run(ncB3, imsB3)
    return _asm_tok([resB3.results[c]["xo3"] for c in range(8)], 1024,
                    np.float32)


_PROGS = {}


def _cached(name, builder):
    if name not in _PROGS:
        _PROGS[name] = builder()
    return _PROGS[name]


def _colpack(v, n):
    return np.ascontiguousarray(np.asarray(v, dtype=np.float32).reshape(n, 128).T)


def _run(nc, ims):
    return run_bass_kernel_spmd(nc, ims, core_ids=list(range(N_CORES)))


def _tok_cols(c):
    b, hf = c // 2, c % 2
    return slice(b * 4096 + hf * 2048, b * 4096 + (hf + 1) * 2048)


def _asm_tok(parts, rows, dtype):
    out = np.zeros((rows, 16384), dtype=dtype)
    for c in range(8):
        out[:, _tok_cols(c)] = parts[c]
    return out


def _mk_fm_rows(t, n=8):
    return [t.ap()[k * 128:(k + 1) * 128, :] for k in range(n)]


def _build_ctx_proj(name, ctx_rows_n, wT_shape, has_bias):
    def b():
        p = Prog(name)
        p.prealloc_consts()
        ctxd = p.dram("ctx", [ctx_rows_n * 128, T_CORE], BF16,
                      kind="ExternalInput")
        xd = p.dram("x", [1024, T_CORE], F32, kind="ExternalInput")
        xo = p.dram("xo", [1024, T_CORE], F32, kind="ExternalOutput")
        wf = p.dram("wf", list(wT_shape), F32, kind="ExternalInput")
        wb = p.dram("wb", list(wT_shape), BF16)
        emit_cast_dram(p, "cw", wf, wb, wT_shape[0], wT_shape[1])
        bias_sb = None
        if has_bias:
            bd = p.dram("bias", [128, 8], F32, kind="ExternalInput")
            bias_sb_t = p.sbuf("bias_sb", [128, 8], F32)
            p.dma("sync", bias_sb_t[:, :], bd.ap())
            bias_sb = bias_sb_t[:, :]
        emit_fm_proj_residual(p, "pr", _mk_fm_rows(ctxd, ctx_rows_n),
                              _mk_fm_rows(xd), _mk_fm_rows(xo), wb, bias_sb,
                              T_CORE, kc_n=ctx_rows_n)
        p.finalize_tail()
        return p.build()
    return b


def _apply_ctx_proj(name, X, ctx_full, wT_arr, bias_arr, ctx_rows_n=8):
    nc = _cached(name, _build_ctx_proj(name, ctx_rows_n,
                                       wT_arr.shape, bias_arr is not None))
    ims = []
    for c in range(8):
        im = {"ctx": np.ascontiguousarray(ctx_full[:, _tok_cols(c)]),
              "x": np.ascontiguousarray(X[:, _tok_cols(c)]),
              "wf": np.ascontiguousarray(wT_arr)}
        if bias_arr is not None:
            im["bias"] = bias_arr
        ims.append(im)
    res = _run(nc, ims)
    return _asm_tok([res.results[c]["xo"] for c in range(8)], 1024, np.float32)


def kernel(**inputs):
    I = inputs
    x0 = np.asarray(I["L0_attn_a_x"])
    X = np.ascontiguousarray(x0.transpose(2, 0, 1).reshape(1024, 16384))

    def b_l0aa():
        p = Prog("l0aa")
        p.prealloc_consts()
        x = p.dram("x", [1024, 4096], F32, kind="ExternalInput")
        ctxo = p.dram("ctxo", [512, 4096], BF16, kind="ExternalOutput")
        qkvf = p.dram("qkvf", [1024, 8 * 192], F32, kind="ExternalInput")
        qkvb = p.dram("qkvb", [1024, 8 * 192], BF16)
        emit_cast_dram(p, "cq", qkvf, qkvb, 1024, 8 * 192)
        W = {"qkvT": qkvb,
             "qb": p.dram("qbv", [128, 4], F32, kind="ExternalInput"),
             "kb": p.dram("kbv", [128, 4], F32, kind="ExternalInput"),
             "vb_row": p.dram("vb_row", [1, 512], F32, kind="ExternalInput")}
        emit_l0_attn_a_ctx(p, "a", _mk_fm_rows(x), ctxo, W, S=4096, NH=8)
        p.finalize_tail()
        return p.build()

    nc = _cached("l0aa", b_l0aa)
    qw, qb_, kw, kb_, vw, vb_ = (I["L0_attn_a_q_w"], I["L0_attn_a_q_b"],
                                 I["L0_attn_a_k_w"], I["L0_attn_a_k_b"],
                                 I["L0_attn_a_v_w"], I["L0_attn_a_v_b"])
    ims = []
    for c in range(8):
        b, hf = c // 2, c % 2
        h0 = hf * 8
        qc = np.concatenate([np.asarray(qw)[h0 + i].T for i in range(8)], 1)
        kc = np.concatenate([np.asarray(kw)[h0 + i].T for i in range(8)], 1)
        vc = np.concatenate([np.asarray(vw)[h0 + i].T for i in range(8)], 1)
        qbv = np.zeros((128, 4), dtype=np.float32)
        kbv = np.zeros((128, 4), dtype=np.float32)
        for j in range(4):
            qbv[0:64, j] = np.asarray(qb_)[h0 + 2 * j, 0]
            qbv[64:128, j] = np.asarray(qb_)[h0 + 2 * j + 1, 0]
            kbv[0:64, j] = np.asarray(kb_)[h0 + 2 * j, 0]
            kbv[64:128, j] = np.asarray(kb_)[h0 + 2 * j + 1, 0]
        vbr = np.concatenate([np.asarray(vb_)[h0 + i, 0] for i in range(8)])
        ims.append({"x": np.ascontiguousarray(X[:, b * 4096:(b + 1) * 4096]),
                    "qkvf": np.ascontiguousarray(
                        np.concatenate([qc, kc, vc], 1)),
                    "qbv": qbv, "kbv": kbv,
                    "vb_row": vbr.reshape(1, 512).astype(np.float32)})
    res = _run(nc, ims)
    ctx_full = np.zeros((1024, 16384), dtype=res.results[0]["ctxo"].dtype)
    for c in range(8):
        b, hf = c // 2, c % 2
        ctx_full[hf * 512:(hf + 1) * 512, b * 4096:(b + 1) * 4096] = \
            res.results[c]["ctxo"]
    lb = np.asarray(I["L0_attn_a_l_b"]).reshape(1024)
    X = _apply_ctx_proj("l0aa_pr", X, ctx_full,
                        np.asarray(I["L0_attn_a_l_w"]).T, _colpack(lb, 8))

    def run_l0_ffn(X):
        def b_():
            p = Prog("l0ffn")
            p.prealloc_consts()
            x = p.dram("x", [1024, T_CORE], F32, kind="ExternalInput")
            xo = p.dram("xo", [1024, T_CORE], F32, kind="ExternalOutput")
            W = {}
            for name, shape in (("Sg", (1024, 1024)), ("Su", (1024, 1024)),
                                ("Sd", (1024, 1024)), ("Wg", (16384, 512)),
                                ("Wu", (16384, 512)), ("Wd", (8192, 1024))):
                f = p.dram(name + "_f", list(shape), F32, kind="ExternalInput")
                bb = p.dram(name, list(shape), BF16)
                emit_cast_dram(p, "c" + name, f, bb, shape[0], shape[1])
                W[name] = bb
            W["Wr"] = p.dram("Wr", [1024, 16], F32, kind="ExternalInput")
            emit_l0_ffn(p, "f", _mk_fm_rows(x), _mk_fm_rows(xo), W, T_CORE)
            p.finalize_tail()
            return p.build()
        nc = _cached("l0ffn", b_)
        im0 = {"Sg_f": np.asarray(I["L0_ffn_Sg"]),
               "Su_f": np.asarray(I["L0_ffn_Su"]),
               "Sd_f": np.asarray(I["L0_ffn_Sd"]),
               "Wg_f": np.asarray(I["L0_ffn_Wg"]).reshape(16384, 512),
               "Wu_f": np.asarray(I["L0_ffn_Wu"]).reshape(16384, 512),
               "Wd_f": np.asarray(I["L0_ffn_Wd"]).reshape(8192, 1024),
               "Wr": np.asarray(I["L0_ffn_Wr"])}
        ims = []
        for c in range(8):
            im = dict(im0)
            im["x"] = np.ascontiguousarray(X[:, _tok_cols(c)])
            ims.append(im)
        res = _run(nc, ims)
        return _asm_tok([res.results[c]["xo"] for c in range(8)], 1024,
                        np.float32)

    X = run_l0_ffn(X)

    def b_l0ab():
        p = Prog("l0ab")
        p.prealloc_consts()
        x = p.dram("x", [1024, 4096], F32, kind="ExternalInput")
        ctxo = p.dram("ctxo", [512, 4096], BF16, kind="ExternalOutput")
        qkvf = p.dram("qkvf", [1024, 8 * 192], F32, kind="ExternalInput")
        qkvb = p.dram("qkvb", [1024, 8 * 192], BF16)
        emit_cast_dram(p, "cq", qkvf, qkvb, 1024, 8 * 192)
        emit_l0_attn_b_ctx(p, "a", _mk_fm_rows(x), ctxo, {"qkvT": qkvb},
                           S=4096, NH=8)
        p.finalize_tail()
        return p.build()

    nc = _cached("l0ab", b_l0ab)
    wat = np.asarray(I["L0_attn_b_w_attn"])
    ims = []
    for c in range(8):
        b, hf = c // 2, c % 2
        h0 = hf * 8
        qc = wat[0:1024].T[:, h0 * 64:(h0 + 8) * 64]
        kc = wat[1024:2048].T[:, h0 * 64:(h0 + 8) * 64]
        vc = wat[2048:3072].T[:, h0 * 64:(h0 + 8) * 64]
        ims.append({"x": np.ascontiguousarray(X[:, b * 4096:(b + 1) * 4096]),
                    "qkvf": np.ascontiguousarray(
                        np.concatenate([qc, kc, vc], 1))})
    res = _run(nc, ims)
    ctx_full = np.zeros((1024, 16384), dtype=res.results[0]["ctxo"].dtype)
    for c in range(8):
        b, hf = c // 2, c % 2
        ctx_full[hf * 512:(hf + 1) * 512, b * 4096:(b + 1) * 4096] = \
            res.results[c]["ctxo"]
    X = _apply_ctx_proj("l0ab_pr", X, ctx_full,
                        np.asarray(I["L0_attn_b_w_proj"]).T, None)

    X = run_l0_ffn(X)

    def b_l1aa():
        p = Prog("l1aa")
        p.prealloc_consts()
        x = p.dram("x", [1024, 4096], F32, kind="ExternalInput")
        att = p.dram("att", [512, 4096], F32, kind="ExternalOutput")
        qkvuf = p.dram("qkvuf", [1024, 4096], F32, kind="ExternalInput")
        qkvub_ = p.dram("qkvubf", [1024, 4096], BF16)
        emit_cast_dram(p, "cq", qkvuf, qkvub_, 1024, 4096)
        W = {"qkvuT": qkvub_, "h0": 0,
             "qkvub": p.dram("qkvub", [128, 32], F32, kind="ExternalInput"),
             "theta": p.dram("theta", [128, 4], F32, kind="ExternalInput"),
             "slr": p.dram("slr", [1, 8], F32, kind="ExternalInput"),
             "vbias_row": p.dram("vbias_row", [1, 512], F32,
                                 kind="ExternalInput")}
        emit_l1_attn_a_ctx(p, "a", _mk_fm_rows(x), att, W, S=4096, NH=4)
        p.finalize_tail()
        return p.build()

    nc = _cached("l1aa", b_l1aa)
    qkvu_w = np.asarray(I["L1_attn_a_qkvu_w"])
    qkvu_b = np.asarray(I["L1_attn_a_qkvu_b"])
    theta = np.asarray(I["L1_attn_a_theta"])
    slope = np.asarray(I["L1_attn_a_slope_rate"]).reshape(8)
    theta_cols = theta.reshape(8, 128).T
    ims = []
    for c in range(8):
        b, hf = c // 2, c % 2
        if hf == 0:
            wq_arr = qkvu_w
            qb_arr = qkvu_b
            th = theta_cols[:, 0:4]
            sl = slope
            vbr = qkvu_b[2048:2048 + 512]
        else:
            wq4 = qkvu_w.reshape(4, 8, 128, 1024)
            wq_arr = np.concatenate([wq4[:, 4:], wq4[:, :4]], axis=1
                                    ).reshape(4096, 1024)
            qb4 = qkvu_b.reshape(4, 8, 128)
            qb_arr = np.concatenate([qb4[:, 4:], qb4[:, :4]], axis=1
                                    ).reshape(4096)
            th = theta_cols[:, 4:8]
            sl = np.roll(slope, -4)
            vbr = qkvu_b[2048 + 512:3072]
        qkvub_cols = np.zeros((128, 32), dtype=np.float32)
        for sec in range(4):
            for hh in range(8):
                qkvub_cols[:, sec * 8 + hh] = qb_arr[sec * 1024 + hh * 128:
                                                     sec * 1024 + (hh + 1) * 128]
        ims.append({"x": np.ascontiguousarray(X[:, b * 4096:(b + 1) * 4096]),
                    "qkvuf": np.ascontiguousarray(wq_arr.T),
                    "qkvub": qkvub_cols, "theta": np.ascontiguousarray(th),
                    "slr": sl.reshape(1, 8).astype(np.float32),
                    "vbias_row": vbr.reshape(1, 512).astype(np.float32)})
    res = _run(nc, ims)
    att_full = np.zeros((1024, 16384), dtype=np.float32)
    for c in range(8):
        b, hf = c // 2, c % 2
        att_full[hf * 512:(hf + 1) * 512, b * 4096:(b + 1) * 4096] = \
            res.results[c]["att"]

    def b_l1aa_out():
        p = Prog("l1aa_out")
        p.prealloc_consts()
        attf = p.dram("attf", [1024, T_CORE], F32, kind="ExternalInput")
        xB = p.dram("xB", [1024, T_CORE], F32, kind="ExternalInput")
        xo = p.dram("xo", [1024, T_CORE], F32, kind="ExternalOutput")
        qkvufB = p.dram("qkvufB", [1024, 4096], F32, kind="ExternalInput")
        qkvubB = p.dram("qkvubB", [1024, 4096], BF16)
        emit_cast_dram(p, "cq", qkvufB, qkvubB, 1024, 4096)
        owf = p.dram("owf", [1024, 1024], F32, kind="ExternalInput")
        owb = p.dram("owb", [1024, 1024], BF16)
        emit_cast_dram(p, "cw", owf, owb, 1024, 1024)
        WB = {"qkvuT": qkvubB,
              "ubias": p.dram("ubias", [128, 8], F32, kind="ExternalInput"),
              "outwT": owb,
              "outb": p.dram("outb", [128, 8], F32, kind="ExternalInput")}
        emit_l1_attn_a_out(p, "b", _mk_fm_rows(attf), _mk_fm_rows(xB),
                           _mk_fm_rows(xo), WB, T_CORE)
        p.finalize_tail()
        return p.build()

    nc = _cached("l1aa_out", b_l1aa_out)
    ims = []
    for c in range(8):
        ims.append({"attf": np.ascontiguousarray(att_full[:, _tok_cols(c)]),
                    "xB": np.ascontiguousarray(X[:, _tok_cols(c)]),
                    "qkvufB": np.ascontiguousarray(qkvu_w.T),
                    "ubias": _colpack(qkvu_b[3072:4096], 8),
                    "owf": np.ascontiguousarray(
                        np.asarray(I["L1_attn_a_out_w"]).T),
                    "outb": _colpack(np.asarray(I["L1_attn_a_out_b"]), 8)})
    res = _run(nc, ims)
    X = _asm_tok([res.results[c]["xo"] for c in range(8)], 1024, np.float32)

    def run_l1_ffn(X):
        def b_():
            p = Prog("l1ffn")
            p.prealloc_consts()
            x = p.dram("x", [1024, T_CORE], F32, kind="ExternalInput")
            w1t = p.dram("w1t", [1024, 4096], F32, kind="ExternalInput")
            w2t = p.dram("w2t", [4096, 1024], F32, kind="ExternalInput")
            b1s = p.dram("b1s", [128, 32], F32, kind="ExternalInput")
            b2s = p.dram("b2s", [128, 8], F32, kind="ExternalInput")
            xo = p.dram("xo", [1024, T_CORE], F32, kind="ExternalOutput")
            qw1 = p.dram("qw1", [1024, 4096], BF16)
            qw2 = p.dram("qw2", [4096, 1024], BF16)
            emit_quantize_w(p, "q", [(w1t, qw1, 1024, 4096),
                                     (w2t, qw2, 4096, 1024)])
            emit_l1_ffn(p, "f", _mk_fm_rows(x), _mk_fm_rows(xo), qw1, qw2,
                        b1s, b2s, T_CORE)
            p.finalize_tail()
            return p.build()
        nc = _cached("l1ffn", b_)
        im0 = {"w1t": np.ascontiguousarray(np.asarray(I["L1_ffn_W1"]).T),
               "w2t": np.ascontiguousarray(np.asarray(I["L1_ffn_W2"]).T),
               "b1s": _colpack(np.asarray(I["L1_ffn_b1"]), 32),
               "b2s": _colpack(np.asarray(I["L1_ffn_b2"]), 8)}
        ims = []
        for c in range(8):
            im = dict(im0)
            im["x"] = np.ascontiguousarray(X[:, _tok_cols(c)])
            ims.append(im)
        res = _run(nc, ims)
        return _asm_tok([res.results[c]["xo"] for c in range(8)], 1024,
                        np.float32)

    X = run_l1_ffn(X)
    X = run_l1_attn_b(I, X)
    X = run_l1_ffn(X)

    def b_l2aa():
        p = Prog("l2aa")
        p.prealloc_consts()
        x = p.dram("x", [1024, 4096], F32, kind="ExternalInput")
        ctxo = p.dram("ctxo", [512, 4096], BF16, kind="ExternalOutput")
        qkvf = p.dram("qkvf", [1024, 32 * 48], F32, kind="ExternalInput")
        qkvb = p.dram("qkvb", [1024, 32 * 48], BF16)
        emit_cast_dram(p, "cq", qkvf, qkvb, 1024, 32 * 48)
        W = {"qkvT": qkvb,
             "qkvb_row": p.dram("qkvb_row", [1, 32 * 48], F32,
                                kind="ExternalInput"),
             "beta": p.dram("betad", [1, 1], F32, kind="ExternalInput")}
        emit_l2_attn_a_ctx(p, "a", _mk_fm_rows(x), ctxo, W, S=4096, NZ=32)
        p.finalize_tail()
        return p.build()

    nc = _cached("l2aa", b_l2aa)
    Wq, Wk, Wv = (np.asarray(I["L2_attn_a_Wq"]), np.asarray(I["L2_attn_a_Wk"]),
                  np.asarray(I["L2_attn_a_Wv"]))
    bq, bk, bv = (np.asarray(I["L2_attn_a_bq"]), np.asarray(I["L2_attn_a_bk"]),
                  np.asarray(I["L2_attn_a_bv"]))
    beta = np.asarray(I["L2_attn_a_beta"]).reshape(1, 1)
    ims = []
    for c in range(8):
        b, hf = c // 2, c % 2
        csl = slice(hf * 512, (hf + 1) * 512)
        qkv_cols = np.concatenate([Wq.T[:, csl], Wk.T[:, csl], Wv.T[:, csl]], 1)
        b_row = np.concatenate([bq[csl], bk[csl], bv[csl]]).reshape(1, -1)
        ims.append({"x": np.ascontiguousarray(X[:, b * 4096:(b + 1) * 4096]),
                    "qkvf": np.ascontiguousarray(qkv_cols),
                    "qkvb_row": b_row.astype(np.float32), "betad": beta})
    res = _run(nc, ims)
    ctx_full = np.zeros((1024, 16384), dtype=res.results[0]["ctxo"].dtype)
    for c in range(8):
        b, hf = c // 2, c % 2
        ctx_full[hf * 512:(hf + 1) * 512, b * 4096:(b + 1) * 4096] = \
            res.results[c]["ctxo"]
    X = _apply_ctx_proj("l2aa_pr", X, ctx_full,
                        np.asarray(I["L2_attn_a_Wo"]).T,
                        _colpack(np.asarray(I["L2_attn_a_bo"]), 8))

    def run_l2_ffn(X):
        def b_():
            p = Prog("l2ffn")
            p.prealloc_consts()
            x = p.dram("x", [1024, T_CORE], F32, kind="ExternalInput")
            xo = p.dram("xo", [1024, T_CORE], F32, kind="ExternalOutput")
            W = {}
            for name, shape in (("w1", (8192, 1408)), ("w3", (8192, 1408)),
                                ("w2", (11264, 1024)), ("sw1", (1024, 1408)),
                                ("sw3", (1024, 1408)), ("sw2", (1408, 1024))):
                f = p.dram(name + "_f", list(shape), F32, kind="ExternalInput")
                bb = p.dram(name, list(shape), BF16)
                emit_cast_dram(p, "c" + name, f, bb, shape[0], shape[1])
                W[name] = bb
            for name, shape in (("gate_wT", (1024, 8)), ("gate_b", (1, 8)),
                                ("b13", (128, 176)), ("sb13", (128, 22)),
                                ("sb2", (128, 8)), ("b2", (8, 1024))):
                W[name] = p.dram(name, list(shape), F32, kind="ExternalInput")
            emit_l2_ffn(p, "f", _mk_fm_rows(x), _mk_fm_rows(xo), W, T_CORE)
            p.finalize_tail()
            return p.build()
        nc = _cached("l2ffn", b_)
        b1 = np.asarray(I["L2_ffn_b1"])
        b3 = np.asarray(I["L2_ffn_b3"])
        b13 = np.zeros((128, 176), dtype=np.float32)
        for e in range(8):
            b13[:, e * 22:e * 22 + 11] = _colpack(b1[e], 11)
            b13[:, e * 22 + 11:e * 22 + 22] = _colpack(b3[e], 11)
        sb13 = np.concatenate([_colpack(np.asarray(I["L2_ffn_sb1"]), 11),
                               _colpack(np.asarray(I["L2_ffn_sb3"]), 11)], 1)
        im0 = {"w1_f": np.asarray(I["L2_ffn_w1"]).reshape(8192, 1408),
               "w3_f": np.asarray(I["L2_ffn_w3"]).reshape(8192, 1408),
               "w2_f": np.asarray(I["L2_ffn_w2"]).reshape(11264, 1024),
               "sw1_f": np.asarray(I["L2_ffn_sw1"]),
               "sw3_f": np.asarray(I["L2_ffn_sw3"]),
               "sw2_f": np.asarray(I["L2_ffn_sw2"]),
               "gate_wT": np.ascontiguousarray(
                   np.asarray(I["L2_ffn_gate_w"]).T),
               "gate_b": np.asarray(I["L2_ffn_gate_b"]).reshape(1, 8),
               "b13": b13, "sb13": np.ascontiguousarray(sb13),
               "sb2": _colpack(np.asarray(I["L2_ffn_sb2"]), 8),
               "b2": np.asarray(I["L2_ffn_b2"])}
        ims = []
        for c in range(8):
            im = dict(im0)
            im["x"] = np.ascontiguousarray(X[:, _tok_cols(c)])
            ims.append(im)
        res = _run(nc, ims)
        return _asm_tok([res.results[c]["xo"] for c in range(8)], 1024,
                        np.float32)

    X = run_l2_ffn(X)
    X = run_mamba(I, X, "L2")
    X = run_l2_ffn(X)

    def b_l3aa():
        p = Prog("l3aa")
        p.prealloc_consts()
        x = p.dram("x", [1024, 4096], F32, kind="ExternalInput")
        ctxo = p.dram("ctxo", [512, 4096], BF16, kind="ExternalOutput")
        qkvw_f = p.dram("qkvw_f", [1024, 1536], F32, kind="ExternalInput")
        qkvw = p.dram("qkvw", [1024, 1536], BF16)
        emit_cast_dram(p, "cq", qkvw_f, qkvw, 1024, 1536)
        W = {"qkvw": qkvw,
             "qb": p.dram("qb", [128, 4], F32, kind="ExternalInput"),
             "kb": p.dram("kb", [128, 4], F32, kind="ExternalInput"),
             "vb_row": p.dram("vb_row", [1, 512], F32, kind="ExternalInput")}
        emit_l3_attn_a_ctx(p, "a", _mk_fm_rows(x), ctxo, W, S=4096, NH=8)
        p.finalize_tail()
        return p.build()

    nc = _cached("l3aa", b_l3aa)
    qkv_w = np.asarray(I["L3_attn_a_qkv_w"])
    qkv_b = np.asarray(I["L3_attn_a_qkv_b"])
    ims = []
    for c in range(8):
        b, hf = c // 2, c % 2
        h0 = hf * 8
        cols = (list(range(h0 * 64, (h0 + 8) * 64)) +
                list(range(1024 + h0 * 64, 1024 + (h0 + 8) * 64)) +
                list(range(2048 + h0 * 64, 2048 + (h0 + 8) * 64)))
        qbs = qkv_b[h0 * 64:(h0 + 8) * 64]
        kbs = qkv_b[1024 + h0 * 64:1024 + (h0 + 8) * 64]
        vbs = qkv_b[2048 + h0 * 64:2048 + (h0 + 8) * 64]
        ims.append({"x": np.ascontiguousarray(X[:, b * 4096:(b + 1) * 4096]),
                    "qkvw_f": np.ascontiguousarray(qkv_w[:, cols]),
                    "qb": _colpack(qbs, 4), "kb": _colpack(kbs, 4),
                    "vb_row": vbs.reshape(1, 512).astype(np.float32)})
    res = _run(nc, ims)
    ctx_full = np.zeros((1024, 16384), dtype=res.results[0]["ctxo"].dtype)
    for c in range(8):
        b, hf = c // 2, c % 2
        ctx_full[hf * 512:(hf + 1) * 512, b * 4096:(b + 1) * 4096] = \
            res.results[c]["ctxo"]
    X = _apply_ctx_proj("l3aa_pr", X, ctx_full,
                        np.asarray(I["L3_attn_a_out_w"]),
                        _colpack(np.asarray(I["L3_attn_a_out_b"]), 8))

    def run_l3_ffn(X):
        def b_():
            p = Prog("l3ffn")
            p.prealloc_consts()
            x = p.dram("x", [1024, 4 * 512], F32, kind="ExternalInput")
            xo = p.dram("xo", [1024, 4 * 512], F32, kind="ExternalOutput")
            names = ["wf", "uf", "wi", "ui", "wc", "uc", "wo", "uo"]
            W = {}
            for n in names:
                wf32 = p.dram(n + "_f", [1024, 1024], F32,
                              kind="ExternalInput")
                wbb = p.dram(n + "_b", [1024, 1024], BF16)
                emit_cast_dram(p, "c" + n, wf32, wbb, 1024, 1024)
                W[n] = wbb
            W["bias"] = p.dram("bias", [128, 32], F32, kind="ExternalInput")
            W["in_c"] = p.dram("in_c", [128, 8], F32, kind="ExternalInput")
            emit_l3_ffn(p, "f", x, xo, W, 512)
            p.finalize_tail()
            return p.build()
        nc = _cached("l3ffn", b_)
        names = ["wf", "uf", "wi", "ui", "wc", "uc", "wo", "uo"]
        bias_arr = np.concatenate(
            [_colpack(np.asarray(I["L3_ffn_b" + g]), 8) for g in
             ("f", "i", "c", "o")], axis=1)
        im0 = {"bias": np.ascontiguousarray(bias_arr),
               "in_c": _colpack(np.asarray(I["L3_ffn_in_c"]), 8)}
        for n in names:
            im0[n + "_f"] = np.ascontiguousarray(
                np.asarray(I["L3_ffn_" + n]).T)
        ims = []
        cols_of = []
        for c in range(8):
            cols = [slice(b * 4096 + c * 512, b * 4096 + (c + 1) * 512)
                    for b in range(4)]
            cols_of.append(cols)
            im = dict(im0)
            im["x"] = np.ascontiguousarray(
                np.concatenate([X[:, sl] for sl in cols], axis=1))
            ims.append(im)
        res = _run(nc, ims)
        Xn = np.zeros_like(X)
        for c in range(8):
            o = res.results[c]["xo"]
            for b in range(4):
                Xn[:, cols_of[c][b]] = o[:, b * 512:(b + 1) * 512]
        return Xn

    X = run_l3_ffn(X)
    X = run_mamba(I, X, "L3")
    X = run_l3_ffn(X)

    return np.ascontiguousarray(
        X.reshape(1024, 4, 4096).transpose(1, 2, 0)).astype(np.float32)
